# version 01
# baseline (speedup 1.0000x reference)
import math
from contextlib import ExitStack
from functools import lru_cache

import numpy as np
import ml_dtypes

import concourse.bass as bass
import concourse.mybir as mybir
from concourse.bass_utils import run_bass_kernel_spmd

F32 = mybir.dt.float32
BF16 = mybir.dt.bfloat16
AF = mybir.ActivationFunctionType
ALU = mybir.AluOpType
AX = mybir.AxisListType

NCORES = 8
B, S, D = 2, 4096, 2048
T = B * S
TPC = T // NCORES
NT = TPC // 128
DCH = D // 128
LN_EPS = 1e-5
DEBUG_LNQKV = False
BF = ml_dtypes.bfloat16
CORES = list(range(NCORES))



class Ch:

    def __init__(self, handle):
        self.h = handle
        self.n = 0

    def reserve(self, k):
        self.n += k
        return self.n


class Prog:
    ENGINES = ("sync", "scalar", "vector", "tensor", "gpsimd")

    def __init__(self, name):
        self.nc = bass.Bass("TRN2", target_bir_lowering=False, name=name,
                            detect_race_conditions=False)
        self.ctx = ExitStack()
        self.ops = {e: [] for e in self.ENGINES}
        self.chans = {}

    def sbuf(self, name, shape, dt=F32):
        return self.ctx.enter_context(self.nc.sbuf_tensor(name, shape, dt))

    def psum(self, name, shape, dt=F32):
        return self.ctx.enter_context(self.nc.psum_tensor(name, shape, dt))

    def chan(self, name):
        if name not in self.chans:
            self.chans[name] = Ch(self.ctx.enter_context(self.nc.semaphore(name)))
        return self.chans[name]

    def dram_in(self, name, shape, dt=F32):
        return self.nc.dram_tensor(name, list(shape), dt, kind="ExternalInput").ap()

    def dram_out(self, name, shape, dt=F32):
        return self.nc.dram_tensor(name, list(shape), dt, kind="ExternalOutput").ap()

    def raw(self, eng, fn):
        self.ops[eng].append(fn)

    def wait(self, eng, ch, tok):
        if tok > 0:
            self.ops[eng].append(lambda e, _h=ch.h, _t=tok: e.wait_ge(_h, _t))

    def dma(self, eng, out, in_, ch, **kw):
        tok = ch.reserve(16)
        self.ops[eng].append(
            lambda e, _o=out, _i=in_, _h=ch.h, _kw=kw: e.dma_start(out=_o, in_=_i, **_kw).then_inc(_h, 16)
        )
        return tok

    def step(self, eng, fn, inc=None, k=1):
        tok = inc.reserve(k) if inc is not None else None
        if inc is not None:
            def run(e, _fn=fn, _h=inc.h, _k=k):
                _fn(e).then_inc(_h, _k)
        else:
            def run(e, _fn=fn):
                _fn(e)
        self.ops[eng].append(run)
        return tok

    def finalize(self):
        with self.nc.Block() as block:
            for eng in self.ENGINES:
                fns = self.ops[eng]
                if fns:
                    getattr(block, eng)(lambda e, _fns=tuple(fns): [f(e) for f in _fns])
        self.ctx.close()
        return self.nc


def run(nc, in_maps):
    return run_bass_kernel_spmd(nc, in_maps, core_ids=CORES)



def walk_layernorm_tile(p, x_ap, xn_ap, scratch, col, ch_ln):
    ch_st = p.chan("_ln_stats")
    ssum, ssq, srstd, snb = scratch["ssum"], scratch["ssq"], scratch["srstd"], scratch["snb"]
    c = col
    p.step("vector", lambda e: e.reduce_sum(ssum[:, c:c+1], x_ap, axis=AX.X),
           inc=ch_st)
    t_sq = p.step("scalar", lambda e: e.activation(
        scratch["wide"][:], x_ap, AF.Square, accum_out=ssq[:, c:c+1]), inc=ch_st)
    p.wait("vector", ch_st, t_sq)
    t = p.step("vector", lambda e: e.tensor_scalar_mul(
        ssum[:, c:c+1], ssum[:, c:c+1], 1.0 / D), inc=ch_st)
    p.wait("vector", ch_st, t)
    t = p.step("vector", lambda e: e.tensor_mul(
        scratch["m2"][:, c:c+1], ssum[:, c:c+1], ssum[:, c:c+1]), inc=ch_st)
    p.wait("vector", ch_st, t)
    t = p.step("vector", lambda e: e.scalar_tensor_tensor(
        srstd[:, c:c+1], ssq[:, c:c+1], 1.0 / D, scratch["m2"][:, c:c+1],
        op0=ALU.mult, op1=ALU.subtract), inc=ch_st)
    p.wait("vector", ch_st, t)
    t_var = p.step("vector", lambda e: e.tensor_scalar_add(
        srstd[:, c:c+1], srstd[:, c:c+1], LN_EPS), inc=ch_st)
    p.wait("scalar", ch_st, t_var)
    t = p.step("scalar", lambda e: e.activation(
        srstd[:, c:c+1], srstd[:, c:c+1], AF.Ln), inc=ch_st)
    p.wait("scalar", ch_st, t)
    t_rstd = p.step("scalar", lambda e: e.activation(
        srstd[:, c:c+1], srstd[:, c:c+1], AF.Exp, scale=-0.5), inc=ch_st)
    p.wait("vector", ch_st, t_rstd)
    t_nb = p.step("vector", lambda e: e.scalar_tensor_tensor(
        snb[:, c:c+1], ssum[:, c:c+1], -1.0, srstd[:, c:c+1],
        op0=ALU.mult, op1=ALU.mult), inc=ch_st)
    p.wait("scalar", ch_st, t_nb)
    return p.step("scalar", lambda e: e.activation(
        xn_ap, x_ap, AF.Identity, bias=snb[:, c:c+1], scale=srstd[:, c:c+1]),
        inc=ch_ln)


def ln_scratch(p):
    return {
        "ssum": p.sbuf("_ssum", (128, NT)), "ssq": p.sbuf("_ssq", (128, NT)),
        "srstd": p.sbuf("_srstd", (128, NT)), "snb": p.sbuf("_snb", (128, NT)),
        "m2": p.sbuf("_m2", (128, NT)), "wide": p.sbuf("_wide", (128, D)),
    }


def walk_ln_all(p, x_sb, xn_sb, ch_ln):
    scratch = ln_scratch(p)
    return [walk_layernorm_tile(p, x_sb[i][:], xn_sb[i][:], scratch, i, ch_ln)
            for i in range(NT)]


def walk_transpose_xn(p, xn_sb, xnf_sb, ident, ch_ln, ln_toks, name="tr"):
    ch_tr = p.chan(name)
    ch_trev = p.chan(name + "_ev")
    pT = [p.psum(name + f"_p{j}", (128, 128), BF16) for j in range(2)]
    toks = [[None] * NT for _ in range(DCH)]
    cnt = 0
    for i in range(NT):
        if ln_toks is not None:
            p.wait("tensor", ch_ln, ln_toks[i])
        for d in range(DCH):
            bank = cnt % 2
            if cnt >= 2:
                p.wait("tensor", ch_trev, cnt - 1)
            t_tr = p.step("tensor", lambda e, _i=i, _d=d, _b=bank: e.transpose(
                pT[_b][:], xn_sb[_i][:, _d*128:(_d+1)*128], ident[:]), inc=ch_tr)
            p.wait("vector", ch_tr, t_tr)
            toks[d][i] = p.step("vector", lambda e, _i=i, _d=d, _b=bank: e.tensor_copy(
                xnf_sb[_d][:, _i*128:(_i+1)*128], pT[_b][:]), inc=ch_trev)
            cnt += 1
    return toks



@lru_cache(maxsize=None)
def build_lnqkv(with_rope: bool):
    p = Prog("lnqkv_rope" if with_rope else "lnqkv")
    x = p.dram_in("x", (TPC, D))
    ws = [p.dram_in(n, (D, D)) for n in ("wq", "wk", "wv")]
    identin = p.dram_in("ident", (128, 128), BF16)
    if with_rope:
        rc = p.dram_in("rope_cos", (TPC, D // 2), BF16)
        rs = p.dram_in("rope_sin", (TPC, D // 2), BF16)
    outs = [p.dram_out(n, (TPC, D), BF16) for n in ("q", "k", "v")]
    dbg = DEBUG_LNQKV
    if dbg:
        xn_dbg = p.dram_out("xn_dbg", (TPC, D), BF16)
        w_dbg = p.dram_out("w_dbg", (128, DCH * D), BF16)

    x_sb = [p.sbuf(f"x{j}", (128, D)) for j in range(2)]
    xn_sb = [p.sbuf(f"xn{j}", (128, D), BF16) for j in range(2)]
    xnf_sb = [p.sbuf(f"xnf{d}", (128, TPC), BF16) for d in range(DCH)]
    ident = p.sbuf("identsb", (128, 128), BF16)
    w_sb = p.sbuf("wsb", (128, DCH * D), BF16)
    o32 = [p.sbuf(f"o32{j}", (128, D)) for j in range(2)]
    tmp = [p.sbuf(f"tmp{j}", (128, D // 2)) for j in range(3)]
    if with_rope:
        rc_sb = [p.sbuf(f"rc{i}", (128, D // 2), BF16) for i in range(NT)]
        rs_sb = [p.sbuf(f"rs{i}", (128, D // 2), BF16) for i in range(NT)]
        rc32 = p.sbuf("rc32", (128, D // 2))
        rs32 = p.sbuf("rs32", (128, D // 2))
    scratch = ln_scratch(p)
    pT = [p.psum(f"pT{j}", (128, 1024), BF16) for j in range(2)]
    pO = [p.psum(f"pO{j}", (128, 512)) for j in range(4)]

    ch_tab = p.chan("tab"); ch_w = p.chan("w"); ch_ln = p.chan("ln")
    ch_tr = p.chan("tr"); ch_trev = p.chan("trev")
    ch_mm = p.chan("mm"); ch_ev = p.chan("ev"); ch_rope = p.chan("rope")
    ch_x = [p.chan("x0"), p.chan("x1")]
    ch_st = [p.chan("st0"), p.chan("st1")]

    p.dma("sync", ident[:], identin, ch_tab)
    if with_rope:
        for i in range(NT):
            p.dma("sync", rc_sb[i][:], rc[i*128:(i+1)*128, :], ch_tab)
            p.dma("sync", rs_sb[i][:], rs[i*128:(i+1)*128, :], ch_tab)
    ld_tabs = ch_tab.n

    tr_cnt = 0
    ln_toks = []
    for i in range(NT):
        xb, nb = x_sb[i % 2], xn_sb[i % 2]
        if i >= 2:
            p.wait("sync", ch_ln, ln_toks[i - 2])
        t_x = p.dma("sync", xb[:], x[i*128:(i+1)*128, :], ch_x[i % 2])
        p.wait("vector", ch_x[i % 2], t_x)
        p.wait("scalar", ch_x[i % 2], t_x)
        if i >= 2:
            p.wait("scalar", ch_tr, (i - 1) * DCH)
        t_ln = walk_layernorm_tile(p, xb[:], nb[:], scratch, i, ch_ln)
        ln_toks.append(t_ln)
        if dbg:
            p.wait("sync", ch_ln, t_ln)
            p.dma("sync", xn_dbg[i*128:(i+1)*128, :], nb[:], ch_tab)
        p.wait("tensor", ch_ln, t_ln)
        if i == 0:
            p.wait("tensor", ch_tab, ld_tabs)
            p.wait("vector", ch_tab, ld_tabs)
        for d in range(DCH):
            bank = tr_cnt % 2
            if tr_cnt >= 2:
                p.wait("tensor", ch_trev, tr_cnt - 1)
            t_tr = p.step("tensor", lambda e, _n=nb, _d=d, _b=bank: e.transpose(
                pT[_b][:, 0:128], _n[:, _d*128:(_d+1)*128], ident[:]), inc=ch_tr)
            p.wait("vector", ch_tr, t_tr)
            p.step("vector", lambda e, _i=i, _d=d, _b=bank: e.tensor_copy(
                xnf_sb[_d][:, _i*128:(_i+1)*128], pT[_b][:, 0:128]), inc=ch_trev)
            tr_cnt += 1
    tr_all = ch_trev.n

    cnt = 0
    for m in range(3):
        p.wait("gpsimd", ch_mm, m * NT * 4)
        t_w = p.dma("gpsimd", w_sb.ap().rearrange("p (d c) -> p d c", d=DCH),
                    ws[m].rearrange("(d p) o -> p d o", p=128), ch_w)
        p.wait("tensor", ch_w, t_w)
        if dbg and m == 0:
            p.wait("sync", ch_w, t_w)
            p.dma("sync", w_dbg, w_sb[:], ch_tab)
        if m == 0:
            p.wait("tensor", ch_trev, tr_all)
        for i in range(NT):
            for oc in range(4):
                bank = cnt % 4
                if cnt >= 4:
                    p.wait("tensor", ch_ev, cnt - 3)
                for d in range(DCH):
                    p.step("tensor", lambda e, _d=d, _i=i, _oc=oc, _b=bank,
                           _s=(d == 0), _last=(d == DCH - 1):
                           e.matmul(pO[_b][:], xnf_sb[_d][:, _i*128:(_i+1)*128],
                                    w_sb[:, _d*D + _oc*512:_d*D + (_oc+1)*512],
                                    start=_s, stop=_last),
                           inc=ch_mm if d == DCH - 1 else None)
                cnt += 1
                p.wait("vector", ch_mm, cnt)
                ti = cnt // 4 - 1 + (1 if cnt % 4 else 0)
                ti = (cnt - 1) // 4
                if ti >= 2 and oc == 0:
                    p.wait("vector", ch_st[ti % 2], 16 * (ti // 2))
                p.step("vector", lambda e, _i=i, _oc=oc, _b=bank: e.tensor_copy(
                    o32[_i % 2][:, _oc*512:(_oc+1)*512], pO[_b][:]), inc=ch_ev)
            if with_rope and m < 2:
                def rope(e, _i=i):
                    ot = o32[_i % 2]
                    pair = ot.ap().rearrange("p (c two) -> p c two", two=2)
                    q_e, q_o = pair[:, :, 0:1], pair[:, :, 1:2]
                    e.tensor_copy(rc32[:], rc_sb[_i][:])
                    e.tensor_copy(rs32[:], rs_sb[_i][:])
                    c_e = rc32.ap().rearrange("p (c one) -> p c one", one=1)
                    s_e = rs32.ap().rearrange("p (c one) -> p c one", one=1)
                    def tv(t):
                        return t.ap().rearrange("p (c one) -> p c one", one=1)
                    e.tensor_mul(tv(tmp[0]), q_e, s_e)
                    e.tensor_mul(tv(tmp[1]), q_o, s_e)
                    e.tensor_mul(tv(tmp[2]), q_o, c_e)
                    e.tensor_mul(q_e, q_e, c_e)
                    e.tensor_sub(q_e, q_e, tv(tmp[1]))
                    return e.tensor_add(q_o, tv(tmp[2]), tv(tmp[0]))
                p.step("vector", rope, inc=ch_rope)
            else:
                p.step("vector", lambda e: e.nop(), inc=ch_rope)
            p.wait("gpsimd", ch_rope, ch_rope.n)
            gti = m * NT + i
            p.dma("gpsimd", outs[m][i*128:(i+1)*128, :], o32[i % 2][:], ch_st[gti % 2])
    p.wait("gpsimd", ch_st[0], ch_st[0].n)
    p.wait("gpsimd", ch_st[1], ch_st[1].n)
    return p.finalize()



def _bf(x):
    return np.ascontiguousarray(x, dtype=BF) if x.dtype != BF else np.ascontiguousarray(x)


def _f32(x):
    return np.ascontiguousarray(x, dtype=np.float32)


IDENT = np.eye(128, dtype=BF)


def rope_tables_interleaved():
    dk = 128
    inv = np.exp(-math.log(10000.0) * np.arange(0, dk, 2, dtype=np.float64) / dk)
    pos = np.arange(S, dtype=np.float64)
    fr = pos[:, None] * inv[None, :]
    cos = np.cos(fr).astype(np.float32)
    sin = np.sin(fr).astype(np.float32)
    cos_full = np.tile(cos, (1, D // dk))
    sin_full = np.tile(sin, (1, D // dk))
    return cos_full, sin_full


def shard_tokens(x2d):
    return [np.ascontiguousarray(x2d[c*TPC:(c+1)*TPC]) for c in range(NCORES)]



@lru_cache(maxsize=None)
def build_flash(nq: int, nkv: int, scale: float):
    p = Prog(f"flash_{nq}_{nkv}_{scale:.6f}")
    qf = p.dram_in("qf", (nq, 128, S), BF16)
    kf = p.dram_in("kf", (nkv, 128, S), BF16)
    vb = p.dram_in("vb", (nkv, S, 132), BF16)
    identin = p.dram_in("ident", (128, 128), BF16)
    maskin = p.dram_in("mask", (128, 128))
    ctx = p.dram_out("ctx", (nq, S, 128), BF16)

    NKT = S // 128
    q_sb = [p.sbuf(f"q{i}", (128, S), BF16) for i in range(nq)]
    k_sb = [p.sbuf(f"k{i}", (128, S), BF16) for i in range(nkv)]
    v_sb = [p.sbuf(f"v{i}", (128, NKT * 132), BF16) for i in range(nkv)]
    ident = p.sbuf("identsb", (128, 128), BF16)
    mask = p.sbuf("masksb", (128, 128))
    pbuf = p.sbuf("pbuf", (128, 512), BF16)
    ptbuf = [p.sbuf(f"pt{j}", (128, 512), BF16) for j in range(2)]
    mneg = p.sbuf("mneg", (128, 8))
    bias = p.sbuf("bias", (128, 1))
    rinv = p.sbuf("rinv", (128, 1))
    octx = [p.sbuf(f"octx{j}", (128, 128), BF16) for j in range(2)]

    psA = [p.psum(f"psA{j}", (128, 512)) for j in range(2)]
    pPT = [p.psum(f"pPT{j}", (128, 1024), BF16) for j in range(2)]
    pAV = p.psum("pAV", (128, 512))

    ch_ld = p.chan("ld"); ch_mmA = p.chan("mmA"); ch_rmax = p.chan("rmax")
    ch_bias = p.chan("bias"); ch_mmB = p.chan("mmB"); ch_msk = p.chan("msk")
    ch_exp = p.chan("exp"); ch_tr = p.chan("tr"); ch_ptev = p.chan("ptev")
    ch_av = p.chan("av"); ch_fin = p.chan("fin")
    ch_st = [p.chan("st0"), p.chan("st1")]

    for i in range(nq):
        p.dma("sync", q_sb[i][:], qf[i], ch_ld)
    for i in range(nkv):
        p.dma("sync", k_sb[i][:], kf[i], ch_ld)
        p.dma("sync", v_sb[i].ap().rearrange("p (n c) -> p n c", c=132),
              vb[i].rearrange("(n p) c -> p n c", p=128), ch_ld)
    p.dma("sync", ident[:], identin, ch_ld)
    p.dma("sync", mask[:], maskin, ch_ld)
    ld_all = ch_ld.n
    for eng in ("vector", "scalar", "tensor"):
        p.wait(eng, ch_ld, ld_all)

    grp = nq // nkv
    for qi in range(nq):
        kv = qi // grp
        for qt in range(NKT):
            nkt = qt + 1
            nkb = (nkt + 3) // 4
            qslc = q_sb[qi][:, qt*128:(qt+1)*128]
            for kb in range(nkb):
                k0, k1 = kb * 4, min(kb * 4 + 4, nkt)
                kw = (k1 - k0) * 128
                bank = kb % 2
                if kb >= 2:
                    p.wait("tensor", ch_rmax, ch_rmax.n - 1)
                t_mm = p.step("tensor", lambda e, _q=qslc, _kv=kv, _k0=k0, _kw=kw, _b=bank:
                              e.matmul(psA[_b][:, 0:_kw], _q,
                                       k_sb[_kv][:, _k0*128:_k0*128+_kw],
                                       start=True, stop=True), inc=ch_mmA)
                p.wait("vector", ch_mmA, t_mm)
                last = (kb == nkb - 1)
                if last:
                    p.step("vector", lambda e, _b=bank, _kw=kw: e.tensor_add(
                        psA[_b][:, _kw-128:_kw], psA[_b][:, _kw-128:_kw], mask[:]))
                p.step("vector", lambda e, _b=bank, _kw=kw, _kb=kb: e.reduce_max(
                    mneg[:, _kb:_kb+1], psA[_b][:, 0:_kw], axis=AX.X, negate=True),
                    inc=ch_rmax)
            p.wait("vector", ch_rmax, ch_rmax.n)
            if nkb > 1:
                t_red = p.step("vector", lambda e, _n=nkb: e.tensor_reduce(
                    bias[:], mneg[:, 0:_n], axis=AX.X, op=ALU.min), inc=ch_bias)
                p.wait("vector", ch_bias, t_red)
                t_bias = p.step("vector", lambda e: e.tensor_scalar_mul(
                    bias[:], bias[:], float(scale)), inc=ch_bias)
            else:
                t_bias = p.step("vector", lambda e: e.tensor_scalar_mul(
                    bias[:], mneg[:, 0:1], float(scale)), inc=ch_bias)
            p.wait("scalar", ch_bias, t_bias)
            p.wait("tensor", ch_rmax, ch_rmax.n)
            for kb in range(nkb):
                k0, k1 = kb * 4, min(kb * 4 + 4, nkt)
                kw = (k1 - k0) * 128
                bank = kb % 2
                if kb >= 2:
                    p.wait("tensor", ch_exp, ch_exp.n - 1)
                t_mm = p.step("tensor", lambda e, _q=qslc, _kv=kv, _k0=k0, _kw=kw, _b=bank:
                              e.matmul(psA[_b][:, 0:_kw], _q,
                                       k_sb[_kv][:, _k0*128:_k0*128+_kw],
                                       start=True, stop=True), inc=ch_mmB)
                last = (kb == nkb - 1)
                if last:
                    p.wait("vector", ch_mmB, t_mm)
                    t_msk = p.step("vector", lambda e, _b=bank, _kw=kw: e.tensor_add(
                        psA[_b][:, _kw-128:_kw], psA[_b][:, _kw-128:_kw], mask[:]),
                        inc=ch_msk)
                    p.wait("scalar", ch_msk, t_msk)
                else:
                    p.wait("scalar", ch_mmB, t_mm)
                t_exp = p.step("scalar", lambda e, _b=bank, _kw=kw: e.activation(
                    pbuf[:, 0:_kw], psA[_b][:, 0:_kw], AF.Exp,
                    bias=bias[:, 0:1], scale=float(scale)), inc=ch_exp)
                p.wait("tensor", ch_exp, t_exp)
                for j in range(k1 - k0):
                    tb = (kb * 4 + j) % 2
                    cnt_t = ch_tr.n + 1
                    if cnt_t > 2:
                        p.wait("tensor", ch_ptev, ch_ptev.n - 1)
                    t_tr = p.step("tensor", lambda e, _j=j, _tb=tb: e.transpose(
                        pPT[_tb][:, 0:128], pbuf[:, _j*128:(_j+1)*128], ident[:]),
                        inc=ch_tr)
                    p.wait("vector", ch_tr, t_tr)
                    t_ev = p.step("vector", lambda e, _tb=tb: e.tensor_copy(
                        ptbuf[_tb][:, 0:128], pPT[_tb][:, 0:128]), inc=ch_ptev)
                    p.wait("tensor", ch_ptev, t_ev)
                    kt = k0 + j
                    p.step("tensor", lambda e, _tb=tb, _kv=kv, _kt=kt,
                           _s=(kb == 0 and j == 0), _e=(last and j == k1 - k0 - 1):
                           e.matmul(pAV[:, 0:132], ptbuf[_tb][:, 0:128],
                                    v_sb[_kv][:, _kt*132:(_kt+1)*132],
                                    start=_s, stop=_e), inc=ch_av)
            p.wait("vector", ch_av, ch_av.n)
            t_r = p.step("vector", lambda e: e.reciprocal(rinv[:], pAV[:, 128:129]),
                         inc=ch_fin)
            p.wait("vector", ch_fin, t_r)
            ob = octx[qt % 2]
            g = qi * NKT + qt
            if g >= 2:
                p.wait("vector", ch_st[g % 2], 16 * (g // 2))
            t_sc = p.step("vector", lambda e, _ob=ob: e.tensor_scalar(
                _ob[:], pAV[:, 0:128], rinv[:, 0:1], None, op0=ALU.mult), inc=ch_fin)
            p.wait("sync", ch_fin, t_sc)
            p.dma("sync", ctx[qi, qt*128:(qt+1)*128, :], ob[:], ch_st[g % 2])
    p.wait("sync", ch_st[0], ch_st[0].n)
    p.wait("sync", ch_st[1], ch_st[1].n)
    return p.finalize()



@lru_cache(maxsize=None)
def build_projres(in_w: int, lora: bool):
    p = Prog(f"projres_{in_w}_{lora}")
    ICH = in_w // 128
    GRP = 8
    NWG = ICH // GRP
    af = p.dram_in("af", (in_w, TPC), BF16)
    w = p.dram_in("w", (in_w, D))
    x = p.dram_in("x", (TPC, D))
    if lora:
        la = p.dram_in("la", (in_w, 16))
        lb = p.dram_in("lb", (16, D))
    xo = p.dram_out("xo", (TPC, D))

    af_sb = [p.sbuf(f"af{d}", (128, TPC), BF16) for d in range(ICH)]
    w_sb = p.sbuf("wsb", (128, GRP * D), BF16)
    acc = [p.sbuf(f"acc{i}", (128, D)) for i in range(NT)]
    if lora:
        la_sb = p.sbuf("lasb", (128, ICH * 16), BF16)
        lb_sb = p.sbuf("lbsb", (16, D), BF16)
        mid_sb = p.sbuf("mid", (16, TPC), BF16)
        pM = p.psum("pM", (16, TPC))
    pO = [p.psum(f"pO{j}", (128, 512)) for j in range(4)]

    ch_ld = p.chan("ld"); ch_w = p.chan("w"); ch_mm = p.chan("mm")
    ch_ev = p.chan("ev"); ch_mid = p.chan("mid")
    ch_x = p.chan("xc")
    ch_st = [p.chan("sst0"), p.chan("sst1")]

    for d in range(ICH):
        p.dma("sync", af_sb[d][:], af[d*128:(d+1)*128, :], ch_ld)
    for i in range(NT):
        p.dma("sync", acc[i][:], x[i*128:(i+1)*128, :], ch_x)
    if lora:
        p.dma("gpsimd", la_sb.ap().rearrange("p (d c) -> p d c", d=ICH),
              la.rearrange("(d p) o -> p d o", p=128), ch_mid)
        p.dma("gpsimd", lb_sb[:], lb, ch_mid)
    ld_all = ch_ld.n
    x_all = ch_x.n
    p.wait("tensor", ch_ld, ld_all)
    p.wait("vector", ch_x, x_all)

    nmid = 0
    if lora:
        p.wait("tensor", ch_mid, ch_mid.n)
        ch_midc = p.chan("midc")
        for hh in range(2):
            for d in range(ICH):
                p.step("tensor", lambda e, _d=d, _h=hh, _s=(d == 0), _e=(d == ICH - 1):
                       e.matmul(pM[:, _h*512:(_h+1)*512], la_sb[:, _d*16:(_d+1)*16],
                                af_sb[_d][:, _h*512:(_h+1)*512], start=_s, stop=_e),
                       inc=ch_midc if (d == ICH - 1 and hh == 1) else None)
        p.wait("vector", ch_midc, 1)
        t_mev = p.step("vector", lambda e: e.tensor_copy(mid_sb[:], pM[:]), inc=ch_midc)
        p.wait("tensor", ch_midc, t_mev)

    wv = w.rearrange("(d p) o -> p d o", p=128)
    cnt = 0
    for wg in range(NWG):
        p.wait("gpsimd", ch_mm, wg * NT * 4)
        t_w = p.dma("gpsimd", w_sb.ap().rearrange("p (d c) -> p d c", d=GRP),
                    wv[:, wg*GRP:(wg+1)*GRP, :], ch_w)
        p.wait("tensor", ch_w, t_w)
        last_wg = wg == NWG - 1
        for i in range(NT):
            for oc in range(4):
                bank = cnt % 4
                if cnt >= 4:
                    p.wait("tensor", ch_ev, cnt - 3)
                for dd in range(GRP):
                    d = wg * GRP + dd
                    grp_last = dd == GRP - 1 and not (lora and last_wg)
                    p.step("tensor", lambda e, _d=d, _dd=dd, _i=i, _oc=oc, _b=bank,
                           _s=(dd == 0), _st=grp_last:
                           e.matmul(pO[_b][:], af_sb[_d][:, _i*128:(_i+1)*128],
                                    w_sb[:, _dd*D + _oc*512:_dd*D + (_oc+1)*512],
                                    start=_s, stop=_st),
                           inc=ch_mm if grp_last else None)
                if lora and last_wg:
                    p.step("tensor", lambda e, _i=i, _oc=oc, _b=bank:
                           e.matmul(pO[_b][:], mid_sb[:, _i*128:(_i+1)*128],
                                    lb_sb[:, _oc*512:(_oc+1)*512],
                                    start=False, stop=True), inc=ch_mm)
                cnt += 1
                p.wait("vector", ch_mm, cnt)
                p.step("vector", lambda e, _i=i, _oc=oc, _b=bank: e.tensor_add(
                    acc[_i][:, _oc*512:(_oc+1)*512], pO[_b][:],
                    acc[_i][:, _oc*512:(_oc+1)*512]), inc=ch_ev)
    p.wait("sync", ch_ev, ch_ev.n)
    for i in range(NT):
        p.dma("sync", xo[i*128:(i+1)*128, :], acc[i][:], ch_st[i % 2])
    p.wait("sync", ch_st[0], ch_st[0].n)
    p.wait("sync", ch_st[1], ch_st[1].n)
    return p.finalize()



def _wT(w):
    return _f32(np.asarray(w).T)


def rope_tables_L0():
    dk = 128
    inv = np.exp(-math.log(10000.0) * np.arange(0, dk, 2, dtype=np.float64) / dk)
    pos = np.arange(S, dtype=np.float64)
    fr = pos[:, None] * inv[None, :]
    cos = np.tile(np.cos(fr), (1, D // dk)).astype(BF)
    sin = np.tile(np.sin(fr), (1, D // dk)).astype(BF)
    return np.ascontiguousarray(cos), np.ascontiguousarray(sin)


MASK_TILE = np.triu(np.full((128, 128), -1e30, dtype=np.float32), k=1)


def sub_attn_a(x, Wq, Wk, Wv, Wo, with_rope, scale):
    H = 16
    nc1 = build_lnqkv(with_rope)
    im = []
    if with_rope:
        cos, sin = rope_tables_L0()
    for c in range(NCORES):
        m = {"x": _f32(x[c*TPC:(c+1)*TPC]), "wq": _wT(Wq), "wk": _wT(Wk),
             "wv": _wT(Wv), "ident": IDENT}
        if with_rope:
            p0 = (c * TPC) % S
            m["rope_cos"] = np.ascontiguousarray(cos[p0:p0+TPC])
            m["rope_sin"] = np.ascontiguousarray(sin[p0:p0+TPC])
        im.append(m)
    r1 = run(nc1, im).results
    q = np.concatenate([np.asarray(r["q"]) for r in r1])
    k = np.concatenate([np.asarray(r["k"]) for r in r1])
    v = np.concatenate([np.asarray(r["v"]) for r in r1])

    qh = q.reshape(B, S, H, 128)
    kh = k.reshape(B, S, H, 128)
    vh = v.reshape(B, S, H, 128)
    ones_pad = np.zeros((S, 4), dtype=BF)
    ones_pad[:, 0] = 1.0
    nc2 = build_flash(4, 4, scale)
    im2 = []
    for c in range(NCORES):
        pairs = [(pi // H, pi % H) for pi in range(c*4, c*4+4)]
        qf = np.stack([np.ascontiguousarray(qh[b, :, h, :].T) for b, h in pairs])
        kf = np.stack([np.ascontiguousarray(kh[b, :, h, :].T) for b, h in pairs])
        vb = np.stack([np.ascontiguousarray(
            np.concatenate([vh[b, :, h, :], ones_pad], axis=1)) for b, h in pairs])
        im2.append({"qf": qf, "kf": kf, "vb": vb, "ident": IDENT, "mask": MASK_TILE})
    r2 = run(nc2, im2).results
    ctx = np.concatenate([np.asarray(r["ctx"]) for r in r2])
    ctx = ctx.reshape(B, H, S, 128).transpose(0, 2, 1, 3).reshape(T, D)

    nc3 = build_projres(D, False)
    im3 = [{"af": np.ascontiguousarray(ctx[c*TPC:(c+1)*TPC].T),
            "w": _wT(Wo), "x": _f32(x[c*TPC:(c+1)*TPC])}
           for c in range(NCORES)]
    r3 = run(nc3, im3).results
    return np.concatenate([np.asarray(r["xo"]) for r in r3])



@lru_cache(maxsize=None)
def build_quant():
    p = Prog("quant")
    H = 5632
    w_full = [p.dram_in("w1", (H, D)), p.dram_in("w2", (H, D)),
              p.dram_in("w3", (D, H))]
    w_slice = [p.dram_in("w1s", (704, D)), p.dram_in("w2s", (704, D)),
               p.dram_in("w3s", (256, H))]
    q_out = [p.dram_out("qw1", (704, D), BF16), p.dram_out("qw2", (704, D), BF16),
             p.dram_out("qw3", (256, H), BF16)]
    ones = p.sbuf("ones", (128, 1))
    onesr = p.sbuf("onesr", (1, 128))
    tile = [p.sbuf(f"t{j}", (128, H)) for j in range(2)]
    acc = [p.sbuf(f"acc{m}", (128, 44)) for m in range(3)]
    tot = p.sbuf("tot", (128, 3))
    g1 = p.sbuf("g1", (1, 3))
    thr = p.sbuf("thr", (128, 8))
    qa = p.sbuf("qa", (128, H))
    qb = p.sbuf("qb", (128, H))
    qc = p.sbuf("qc", (128, H), BF16)
    pS = p.psum("pS", (1, 4))
    pB = p.psum("pB", (128, 8))
    ch_ld = [p.chan(f"ld{j}") for j in range(2)]
    ch_r = p.chan("r"); ch_g = p.chan("g"); ch_q = p.chan("q")
    ch_st = [p.chan(f"stq{j}") for j in range(2)]

    p.step("gpsimd", lambda e: e.memset(ones[:], 1.0))
    p.step("gpsimd", lambda e: e.memset(onesr[:], 1.0), inc=ch_g)

    specs = [(0, 44, D), (1, 44, D), (2, 16, H)]
    nload = 0
    for m, ntile, width in specs:
        for it in range(ntile):
            b = nload % 2
            if nload >= 2:
                p.wait("sync", ch_r, nload - 1)
            t_l = p.dma("sync", tile[b][:, 0:width],
                        w_full[m][it*128:(it+1)*128, :], ch_ld[b])
            p.wait("vector", ch_ld[b], t_l)
            p.step("vector", lambda e, _b=b, _m=m, _it=it, _w=width: e.reduce_sum(
                acc[_m][:, _it:_it+1], tile[_b][:, 0:_w], axis=AX.X,
                apply_absolute_value=True), inc=ch_r)
            nload += 1
    p.wait("vector", ch_r, ch_r.n)
    for m, ntile, width in specs:
        p.step("vector", lambda e, _m=m, _n=ntile: e.reduce_sum(
            tot[:, _m:_m+1], acc[_m][:, 0:_n], axis=AX.X), inc=ch_r)
    p.wait("tensor", ch_r, ch_r.n)
    p.wait("tensor", ch_g, ch_g.n)
    t_mm = p.step("tensor", lambda e: e.matmul(pS[0:1, 0:3], ones[:], tot[:, 0:3],
                                               start=True, stop=True), inc=ch_g)
    p.wait("vector", ch_g, t_mm)
    t_g = p.step("vector", lambda e: e.tensor_copy(g1[0:1, 0:3], pS[0:1, 0:3]),
                 inc=ch_g)
    p.wait("vector", ch_g, t_g)

    def mkthr(e):
        e.tensor_scalar(g1[0:1, 0:1], g1[0:1, 0:1], 0.5 / float(H * D), None, op0=ALU.mult)
        e.tensor_scalar(g1[0:1, 1:2], g1[0:1, 1:2], 0.5 / float(H * D), None, op0=ALU.mult)
        return e.tensor_scalar(g1[0:1, 2:3], g1[0:1, 2:3], 0.5 / float(H * D), None, op0=ALU.mult)
    t1 = p.step("vector", mkthr, inc=ch_g)
    p.wait("vector", ch_g, t1)
    t2 = p.step("vector", lambda e: e.tensor_scalar_add(g1[0:1, 0:3], g1[0:1, 0:3], 0.5e-8),
                inc=ch_g)
    p.wait("tensor", ch_g, t2)
    t_bc = p.step("tensor", lambda e: e.matmul(pB[:, 0:3], onesr[:], g1[0:1, 0:3],
                                               start=True, stop=True), inc=ch_g)
    p.wait("vector", ch_g, t_bc)
    t_thr = p.step("vector", lambda e: e.tensor_copy(thr[:, 0:3], pB[:, 0:3]), inc=ch_g)
    p.wait("vector", ch_g, t_thr)
    t_nthr = p.step("vector", lambda e: e.tensor_scalar_mul(thr[:, 4:7], thr[:, 0:3], -1.0),
                    inc=ch_g)
    thr_ready = t_nthr

    p.wait("sync", ch_r, ch_r.n)
    q_toks = []
    st_toks = []
    qi = 0
    for m, _, width in specs:
        rows = [128, 128, 128, 128, 128, 64] if m < 2 else [128, 128]
        src_t = w_slice[m]
        dst = q_out[m]
        r0 = 0
        for nr in rows:
            b = qi % 2
            if qi >= 2:
                p.wait("sync", ch_q, q_toks[qi - 2])
            t_l = p.dma("sync", tile[b][0:nr, 0:width], src_t[r0:r0+nr, :], ch_ld[b])
            p.wait("vector", ch_ld[b], t_l)
            p.wait("vector", ch_g, thr_ready)
            if qi >= 1:
                p.wait("vector", ch_st[(qi - 1) % 2], st_toks[qi - 1])

            def quant(e, _b=b, _m=m, _nr=nr, _w=width):
                e.tensor_scalar(qa[0:_nr, 0:_w], tile[_b][0:_nr, 0:_w],
                                thr[0:_nr, _m:_m+1], None, op0=ALU.is_ge)
                e.tensor_scalar(qb[0:_nr, 0:_w], tile[_b][0:_nr, 0:_w],
                                thr[0:_nr, 4+_m:5+_m], None, op0=ALU.is_le)
                return e.tensor_sub(qc[0:_nr, 0:_w], qa[0:_nr, 0:_w], qb[0:_nr, 0:_w])
            t_q = p.step("vector", quant, inc=ch_q)
            q_toks.append(t_q)
            p.wait("sync", ch_q, t_q)
            st_toks.append(p.dma("sync", dst[r0:r0+nr, :], qc[0:nr, 0:width], ch_st[qi % 2]))
            r0 += nr
            qi += 1
    p.wait("sync", ch_st[0], ch_st[0].n)
    p.wait("sync", ch_st[1], ch_st[1].n)
    return p.finalize()



@lru_cache(maxsize=None)
def build_ffn():
    p = Prog("ffn")
    H = 5632
    NHC = H // 512
    x = p.dram_in("x", (TPC, D))
    w1t = p.dram_in("w1t", (D, H), BF16)
    w2t = p.dram_in("w2t", (D, H), BF16)
    w3t = p.dram_in("w3t", (H, D), BF16)
    identin = p.dram_in("ident", (128, 128), BF16)
    xo = p.dram_out("xo", (TPC, D))

    out_acc = [p.sbuf(f"oa{i}", (128, D)) for i in range(NT)]
    xn_sb = [p.sbuf(f"xn{j}", (128, D), BF16) for j in range(2)]
    xnf_sb = [p.sbuf(f"xnf{d}", (128, TPC), BF16) for d in range(DCH)]
    ident = p.sbuf("identsb", (128, 128), BF16)
    w1c = p.sbuf("w1c", (128, DCH * 512), BF16)
    w2c = p.sbuf("w2c", (128, DCH * 512), BF16)
    w3c = p.sbuf("w3c", (128, 4 * D), BF16)
    h1 = p.sbuf("h1", (128, 512))
    hbf = p.sbuf("hbf", (128, 512), BF16)
    hfm = p.sbuf("hfm", (128, 512), BF16)
    scratch = ln_scratch(p)
    pX1 = p.psum("pX1", (128, 512))
    pX2 = p.psum("pX2", (128, 512))
    pTa = p.psum("pTa", (128, 1024), BF16)
    pTb = p.psum("pTb", (128, 1024), BF16)
    pO = [p.psum(f"pOo{j}", (128, 512)) for j in range(4)]

    ch_x = p.chan("chx"); ch_tab = p.chan("tab"); ch_w = p.chan("w")
    ch_ln = p.chan("ln"); ch_tr = p.chan("tr"); ch_trev = p.chan("trev")
    ch_mx = p.chan("mx"); ch_h = p.chan("h"); ch_he = p.chan("he")
    ch_mo = p.chan("mo"); ch_ev = p.chan("ev")
    ch_st = [p.chan("sto0"), p.chan("sto1")]

    p.dma("sync", ident[:], identin, ch_tab)
    for i in range(NT):
        p.dma("sync", out_acc[i][:], x[i*128:(i+1)*128, :], ch_x)
    x_all = ch_x.n
    for eng in ("vector", "scalar"):
        p.wait(eng, ch_x, x_all)
    p.wait("tensor", ch_tab, ch_tab.n)

    tr_cnt = 0
    ln_toks = []
    for i in range(NT):
        nb = xn_sb[i % 2]
        if i >= 2:
            p.wait("scalar", ch_tr, (i - 1) * DCH)
        t_ln = walk_layernorm_tile(p, out_acc[i][:], nb[:], scratch, i, ch_ln)
        ln_toks.append(t_ln)
        p.wait("tensor", ch_ln, t_ln)
        for d in range(DCH):
            bank = tr_cnt % 2
            if tr_cnt >= 2:
                p.wait("tensor", ch_trev, tr_cnt - 1)
            pT = pTa if bank == 0 else pTb
            t_tr = p.step("tensor", lambda e, _n=nb, _d=d, _p=pT: e.transpose(
                _p[:, 0:128], _n[:, _d*128:(_d+1)*128], ident[:]), inc=ch_tr)
            p.wait("vector", ch_tr, t_tr)
            p.step("vector", lambda e, _i=i, _d=d, _p=pT: e.tensor_copy(
                xnf_sb[_d][:, _i*128:(_i+1)*128], _p[:, 0:128]),
                inc=ch_trev)
            tr_cnt += 1
    tr_all = ch_trev.n
    p.wait("tensor", ch_trev, tr_all)

    w1v = w1t.rearrange("(d p) o -> p d o", p=128)
    w2v = w2t.rearrange("(d p) o -> p d o", p=128)
    w3v = w3t.rearrange("(h p) o -> p h o", p=128)
    idx = 0
    for hc in range(NHC):
        p.wait("gpsimd", ch_mo, hc * NT * 4)
        t_w = p.dma("gpsimd", w1c.ap().rearrange("p (d c) -> p d c", d=DCH),
                    w1v[:, :, hc*512:(hc+1)*512], ch_w)
        t_w = p.dma("gpsimd", w2c.ap().rearrange("p (d c) -> p d c", d=DCH),
                    w2v[:, :, hc*512:(hc+1)*512], ch_w)
        t_w = p.dma("gpsimd", w3c.ap().rearrange("p (h o) -> p h o", h=4),
                    w3v[:, hc*4:(hc+1)*4, :], ch_w)
        p.wait("tensor", ch_w, t_w)
        for i in range(NT):
            if idx >= 1:
                p.wait("tensor", ch_h, 2 * idx)
            for d in range(DCH):
                p.step("tensor", lambda e, _d=d, _i=i, _s=(d == 0), _e=(d == DCH - 1):
                       e.matmul(pX1[:], xnf_sb[_d][:, _i*128:(_i+1)*128],
                                w1c[:, _d*512:(_d+1)*512], start=_s, stop=_e),
                       inc=ch_mx if d == DCH - 1 else None)
            for d in range(DCH):
                p.step("tensor", lambda e, _d=d, _i=i, _s=(d == 0), _e=(d == DCH - 1):
                       e.matmul(pX2[:], xnf_sb[_d][:, _i*128:(_i+1)*128],
                                w2c[:, _d*512:(_d+1)*512], start=_s, stop=_e),
                       inc=ch_mx if d == DCH - 1 else None)
            t_mx = ch_mx.n
            p.wait("scalar", ch_mx, t_mx - 1)
            t_sig = p.step("scalar", lambda e: e.activation(h1[:], pX1[:], AF.Sigmoid),
                           inc=ch_h)
            p.wait("vector", ch_mx, t_mx)
            p.wait("vector", ch_h, t_sig)
            t_h = p.step("vector", lambda e: e.tensor_mul(hbf[:], pX2[:], h1[:]),
                         inc=ch_h)
            p.wait("tensor", ch_h, 2 * idx + 2)
            if idx >= 1:
                p.wait("tensor", ch_he, idx)
            for j in range(4):
                p.step("tensor", lambda e, _j=j: e.transpose(
                    pTa[:, _j*128:(_j+1)*128], hbf[:, _j*128:(_j+1)*128],
                    ident[:]), inc=ch_tr if j == 3 else None)
            t_tr4 = ch_tr.n
            p.wait("vector", ch_tr, t_tr4)
            t_he = p.step("vector", lambda e: e.tensor_copy(hfm[:], pTa[:, 0:512]),
                          inc=ch_he)
            p.wait("tensor", ch_he, t_he)
            for osub in range(4):
                if idx >= 1:
                    p.wait("tensor", ch_ev, (idx - 1) * 4 + osub + 1)
                for hsub in range(4):
                    p.step("tensor", lambda e, _h=hsub, _o=osub,
                           _s=(hsub == 0), _e=(hsub == 3):
                           e.matmul(pO[_o][:], hfm[:, _h*128:(_h+1)*128],
                                    w3c[:, _h*D + _o*512:_h*D + (_o+1)*512],
                                    start=_s, stop=_e),
                           inc=ch_mo if hsub == 3 else None)
            mo_base = ch_mo.n
            for osub in range(4):
                p.wait("vector", ch_mo, mo_base - 3 + osub)
                p.step("vector", lambda e, _i=i, _o=osub: e.tensor_add(
                    out_acc[_i][:, _o*512:(_o+1)*512], pO[_o][:],
                    out_acc[_i][:, _o*512:(_o+1)*512]), inc=ch_ev)
            idx += 1
    ev_all = ch_ev.n
    p.wait("sync", ch_ev, ev_all)
    for i in range(NT):
        p.dma("sync", xo[i*128:(i+1)*128, :], out_acc[i][:], ch_st[i % 2])
    p.wait("sync", ch_st[0], ch_st[0].n)
    p.wait("sync", ch_st[1], ch_st[1].n)
    return p.finalize()


def quantize_host_launch(w1, w2, w3):
    nc = build_quant()
    im = []
    for c in range(NCORES):
        im.append({
            "w1": _f32(w1), "w2": _f32(w2), "w3": _f32(w3),
            "w1s": _f32(w1[c*704:(c+1)*704]),
            "w2s": _f32(w2[c*704:(c+1)*704]),
            "w3s": _f32(w3[c*256:(c+1)*256]),
        })
    rs = run(nc, im).results
    qw1 = np.concatenate([np.asarray(r["qw1"]) for r in rs])
    qw2 = np.concatenate([np.asarray(r["qw2"]) for r in rs])
    qw3 = np.concatenate([np.asarray(r["qw3"]) for r in rs])
    return (np.ascontiguousarray(qw1.T), np.ascontiguousarray(qw2.T),
            np.ascontiguousarray(qw3.T))


def sub_ffn(x, qw1t, qw2t, qw3t):
    nc = build_ffn()
    im = [{"x": _f32(x[c*TPC:(c+1)*TPC]), "w1t": qw1t, "w2t": qw2t,
           "w3t": qw3t, "ident": IDENT} for c in range(NCORES)]
    rs = run(nc, im).results
    return np.concatenate([np.asarray(r["xo"]) for r in rs])



@lru_cache(maxsize=None)
def build_dn_pre1():
    p = Prog("dn_pre1")
    MC = 8192
    x = p.dram_in("x", (TPC, D))
    w_qkv = p.dram_in("w_qkv", (D, MC))
    w_z = p.dram_in("w_z", (D, 4096))
    w_ba = p.dram_in("w_ba", (D, 64))
    dtb = p.dram_in("dtb", (128, 32))
    alog = p.dram_in("alog", (128, 32))
    identin = p.dram_in("ident", (128, 128), BF16)
    mixed_o = p.dram_out("mixed", (MC, TPC), BF16)
    z_o = p.dram_out("z", (TPC, 4096), BF16)
    g_o = p.dram_out("g", (TPC, 32))
    beta_o = p.dram_out("beta", (TPC, 32))
    if DEBUG_LNQKV:
        ba_dbg = p.dram_out("ba_dbg", (TPC, 64))
        t1_dbg = p.dram_out("t1_dbg", (TPC, 32))
        ea_dbg = p.dram_out("ea_dbg", (128, 32))

    x_sb = [p.sbuf(f"x{j}", (128, D)) for j in range(2)]
    xn_sb = [p.sbuf(f"xn{j}", (128, D), BF16) for j in range(2)]
    xnf_sb = [p.sbuf(f"xnf{d}", (128, TPC), BF16) for d in range(DCH)]
    ident = p.sbuf("identsb", (128, 128), BF16)
    wq_sb = [p.sbuf(f"wq{j}", (128, DCH * 128), BF16) for j in range(2)]
    wz_sb = p.sbuf("wz", (128, DCH * 512), BF16)
    wba_sb = p.sbuf("wba", (128, DCH * 64), BF16)
    dtb_sb = p.sbuf("dtbsb", (128, 32))
    alog_sb = p.sbuf("alogsb", (128, 32))
    ea_sb = p.sbuf("easb", (128, 32))
    mx_sb = [p.sbuf(f"mx{j}", (128, TPC), BF16) for j in range(2)]
    z_sb = [p.sbuf(f"zsb{j}", (128, 512), BF16) for j in range(2)]
    ba_sb = p.sbuf("basb", (128, 64))
    gb_sb = p.sbuf("gbsb", (128, 64))
    t1_sb = p.sbuf("t1sb", (128, 32))
    t2_sb = p.sbuf("t2sb", (128, 32))
    scratch = ln_scratch(p)
    pTa = p.psum("pTa", (128, 1024), BF16)
    pTb = p.psum("pTb", (128, 1024), BF16)
    pM = [p.psum(f"pM{j}", (128, 512)) for j in range(2)]
    pZ = [p.psum(f"pZ{j}", (128, 512)) for j in range(2)]
    pBA = p.psum("pBA", (128, 64))

    ch_x = [p.chan("x0"), p.chan("x1")]
    ch_tab = p.chan("tab"); ch_wq = [p.chan("wq0"), p.chan("wq1")]
    ch_wz = p.chan("wz"); ch_ln = p.chan("ln"); ch_tr = p.chan("tr")
    ch_trev = p.chan("trev"); ch_mm = p.chan("mm"); ch_ev = p.chan("ev")
    ch_zmm = p.chan("zmm"); ch_zev = p.chan("zev"); ch_ba = p.chan("ba")
    ch_stm = [p.chan("stm0"), p.chan("stm1")]
    ch_stz = [p.chan("stz0"), p.chan("stz1")]
    ch_stg = p.chan("stg")

    p.dma("sync", ident[:], identin, ch_tab)
    p.dma("sync", dtb_sb[:], dtb, ch_tab)
    p.dma("sync", alog_sb[:], alog, ch_tab)
    ch_wba = p.chan("wba")
    t_wba = p.dma("gpsimd", wba_sb.ap().rearrange("p (d c) -> p d c", d=DCH),
                  w_ba.rearrange("(d p) o -> p d o", p=128), ch_wba)
    p.wait("tensor", ch_wba, t_wba)
    tab_all = ch_tab.n
    p.wait("tensor", ch_tab, tab_all)
    p.wait("scalar", ch_tab, tab_all)
    p.wait("vector", ch_tab, tab_all)

    tr_cnt = 0
    ln_toks = []
    for i in range(NT):
        xb, nb = x_sb[i % 2], xn_sb[i % 2]
        if i >= 2:
            p.wait("sync", ch_ln, ln_toks[i - 2])
        t_x = p.dma("sync", xb[:], x[i*128:(i+1)*128, :], ch_x[i % 2])
        p.wait("vector", ch_x[i % 2], t_x)
        p.wait("scalar", ch_x[i % 2], t_x)
        if i >= 2:
            p.wait("scalar", ch_tr, (i - 1) * DCH)
        t_ln = walk_layernorm_tile(p, xb[:], nb[:], scratch, i, ch_ln)
        ln_toks.append(t_ln)
        p.wait("tensor", ch_ln, t_ln)
        for d in range(DCH):
            pT = pTa if tr_cnt % 2 == 0 else pTb
            if tr_cnt >= 2:
                p.wait("tensor", ch_trev, tr_cnt - 1)
            t_tr = p.step("tensor", lambda e, _n=nb, _d=d, _p=pT: e.transpose(
                _p[:, 0:128], _n[:, _d*128:(_d+1)*128], ident[:]), inc=ch_tr)
            p.wait("vector", ch_tr, t_tr)
            p.step("vector", lambda e, _i=i, _d=d, _p=pT: e.tensor_copy(
                xnf_sb[_d][:, _i*128:(_i+1)*128], _p[:, 0:128]), inc=ch_trev)
            tr_cnt += 1
    tr_all = ch_trev.n
    p.wait("tensor", ch_trev, tr_all)

    wqv = w_qkv.rearrange("(d p) o -> p d o", p=128)
    wzv = w_z.rearrange("(d p) o -> p d o", p=128)

    for cc in range(MC // 128):
        buf = cc % 2
        if cc >= 2:
            p.wait("gpsimd", ch_mm, (cc - 1) * 2)
        t_w = p.dma("gpsimd", wq_sb[buf].ap().rearrange("p (d c) -> p d c", d=DCH),
                    wqv[:, :, cc*128:(cc+1)*128], ch_wq[buf])
        p.wait("tensor", ch_wq[buf], t_w)
        for half in range(2):
            bank = (2 * cc + half) % 2
            if 2 * cc + half >= 2:
                p.wait("tensor", ch_ev, 2 * cc + half - 1)
            for d in range(DCH):
                p.step("tensor", lambda e, _d=d, _cc=cc, _h=half, _b=bank,
                       _s=(d == 0), _e=(d == DCH - 1):
                       e.matmul(pM[_b][:], wq_sb[_cc % 2][:, _d*128:(_d+1)*128],
                                xnf_sb[_d][:, _h*512:(_h+1)*512], start=_s, stop=_e),
                       inc=ch_mm if d == DCH - 1 else None)
            t_mm = ch_mm.n
            p.wait("vector", ch_mm, t_mm)
            mb = mx_sb[cc % 2]
            if cc >= 2 and half == 0:
                p.wait("vector", ch_stm[cc % 2], 16 * (cc // 2))
            p.step("vector", lambda e, _b=bank, _h=half, _mb=mb: e.tensor_copy(
                _mb[:, _h*512:(_h+1)*512], pM[_b][:]), inc=ch_ev)
        p.wait("gpsimd", ch_ev, ch_ev.n)
        p.dma("gpsimd", mixed_o[cc*128:(cc+1)*128, :], mx_sb[cc % 2][:], ch_stm[cc % 2])

    for zc in range(4096 // 512):
        if zc >= 1:
            p.wait("gpsimd", ch_zmm, zc * NT)
        t_wz = p.dma("gpsimd", wz_sb.ap().rearrange("p (d c) -> p d c", d=DCH),
                     wzv[:, :, zc*512:(zc+1)*512], ch_wz)
        p.wait("tensor", ch_wz, t_wz)
        for i in range(NT):
            bank = (zc * NT + i) % 2
            if zc * NT + i >= 2:
                p.wait("tensor", ch_zev, zc * NT + i - 1)
            for d in range(DCH):
                p.step("tensor", lambda e, _d=d, _i=i, _b=bank,
                       _s=(d == 0), _e=(d == DCH - 1):
                       e.matmul(pZ[_b][:], xnf_sb[_d][:, _i*128:(_i+1)*128],
                                wz_sb[:, _d*512:(_d+1)*512], start=_s, stop=_e),
                       inc=ch_zmm if d == DCH - 1 else None)
            p.wait("vector", ch_zmm, ch_zmm.n)
            zb = z_sb[(zc * NT + i) % 2]
            gidx = zc * NT + i
            if gidx >= 2:
                p.wait("vector", ch_stz[gidx % 2], 16 * (gidx // 2))
            t_zev = p.step("vector", lambda e, _b=bank, _zb=zb: e.tensor_copy(
                _zb[:], pZ[_b][:]), inc=ch_zev)
            p.wait("gpsimd", ch_zev, t_zev)
            p.dma("gpsimd", z_o[i*128:(i+1)*128, zc*512:(zc+1)*512], zb[:],
                  ch_stz[gidx % 2])

    ch_bam = p.chan("bam")
    copy_toks = []
    for i in range(NT):
        if i >= 1:
            p.wait("tensor", ch_ba, copy_toks[i - 1])
        for d in range(DCH):
            p.step("tensor", lambda e, _d=d, _i=i, _s=(d == 0), _e=(d == DCH - 1):
                   e.matmul(pBA[:], xnf_sb[_d][:, _i*128:(_i+1)*128],
                            wba_sb[:, _d*64:(_d+1)*64], start=_s, stop=_e),
                   inc=ch_bam if d == DCH - 1 else None)
        p.wait("vector", ch_bam, i + 1)
        t_c = p.step("vector", lambda e: e.tensor_copy(ba_sb[:], pBA[:]), inc=ch_ba)
        copy_toks.append(t_c)
        p.wait("scalar", ch_ba, t_c)
        t_bet = p.step("scalar", lambda e: e.activation(
            gb_sb[:, 0:32], ba_sb[:, 0:32], AF.Sigmoid), inc=ch_ba)
        if i == 0:
            t_ea = p.step("scalar", lambda e: e.activation(
                ea_sb[:], alog_sb[:], AF.Exp), inc=ch_ba)
        p.wait("vector", ch_ba, copy_toks[i])
        t = p.step("vector", lambda e: e.tensor_add(
            t1_sb[:], ba_sb[:, 32:64], dtb_sb[:]), inc=ch_ba)
        p.wait("scalar", ch_ba, t)
        t = p.step("scalar", lambda e: e.activation(t2_sb[:], t1_sb[:], AF.Abs), inc=ch_ba)
        p.wait("scalar", ch_ba, t)
        t = p.step("scalar", lambda e: e.activation(t2_sb[:], t2_sb[:], AF.Exp, scale=-1.0), inc=ch_ba)
        p.wait("vector", ch_ba, t)
        t = p.step("vector", lambda e: e.tensor_scalar_add(t2_sb[:], t2_sb[:], 1.0), inc=ch_ba)
        p.wait("scalar", ch_ba, t)
        t = p.step("scalar", lambda e: e.activation(t2_sb[:], t2_sb[:], AF.Ln), inc=ch_ba)
        p.wait("vector", ch_ba, t)
        t = p.step("vector", lambda e: e.tensor_scalar_max(t1_sb[:], t1_sb[:], 0.0), inc=ch_ba)
        p.wait("vector", ch_ba, t)
        t = p.step("vector", lambda e: e.tensor_add(t1_sb[:], t1_sb[:], t2_sb[:]), inc=ch_ba)
        p.wait("vector", ch_ba, t)
        t_g = p.step("vector", lambda e: e.scalar_tensor_tensor(
            gb_sb[:, 32:64], ea_sb[:], -1.0, t1_sb[:], op0=ALU.mult, op1=ALU.mult),
            inc=ch_ba)
        p.wait("sync", ch_ba, ch_ba.n)
        p.dma("sync", beta_o[i*128:(i+1)*128, :], gb_sb[:, 0:32], ch_stg)
        p.dma("sync", g_o[i*128:(i+1)*128, :], gb_sb[:, 32:64], ch_stg)
        if DEBUG_LNQKV:
            p.dma("sync", ba_dbg[i*128:(i+1)*128, :], ba_sb[:], ch_stg)
            p.dma("sync", t1_dbg[i*128:(i+1)*128, :], t1_sb[:], ch_stg)
            if i == 0:
                p.dma("sync", ea_dbg, ea_sb[:], ch_stg)
        p.wait("scalar", ch_stg, ch_stg.n)
        p.wait("vector", ch_stg, ch_stg.n)
    p.wait("gpsimd", ch_stm[0], ch_stm[0].n)
    p.wait("gpsimd", ch_stm[1], ch_stm[1].n)
    p.wait("gpsimd", ch_stz[0], ch_stz[0].n)
    p.wait("gpsimd", ch_stz[1], ch_stz[1].n)
    p.wait("sync", ch_stg, ch_stg.n)
    return p.finalize()



@lru_cache(maxsize=None)
def build_dn_pre2():
    p = Prog("dn_pre2")
    MC = 8192
    mixed = p.dram_in("mixed", (MC, TPC), BF16)
    halo = p.dram_in("halo", (MC, 3), BF16)
    conv_w = p.dram_in("conv_w", (MC, 4))
    convm = p.dram_out("convm", (MC, TPC), BF16)
    if DEBUG_LNQKV:
        mt_dbg = p.dram_out("mt_dbg", (4, 128, 3 + TPC))
        acc_dbg = p.dram_out("acc_dbg", (4, 128, TPC))

    mt = [p.sbuf(f"mt{j}", (128, 3 + TPC)) for j in range(2)]
    mtb = [p.sbuf(f"mtb{j}", (128, 3 + TPC), BF16) for j in range(2)]
    cw = p.sbuf("cw", (128, 4 * 64))
    acc = [p.sbuf(f"acc{j}", (128, TPC)) for j in range(2)]
    ob = [p.sbuf(f"ob{j}", (128, TPC), BF16) for j in range(2)]
    ch_ld = [p.chan("ld0"), p.chan("ld1")]
    ch_cw = p.chan("cw"); ch_cv = p.chan("cv"); ch_sl = p.chan("sl")
    ch_st = [p.chan("st0"), p.chan("st1")]

    t_cw = p.dma("sync", cw.ap().rearrange("p (c f) -> p c f", c=64),
                 conv_w.rearrange("(c p) f -> p c f", p=128), ch_cw)
    p.wait("vector", ch_cw, t_cw)

    cast_toks = []
    silu_toks = []
    for cc in range(64):
        b = cc % 2
        if cc >= 2:
            p.wait("gpsimd", ch_cv, cast_toks[cc - 2])
        p.dma("gpsimd", mtb[b][:, 0:3], halo[cc*128:(cc+1)*128, :], ch_ld[b])
        t_l = p.dma("gpsimd", mtb[b][:, 3:3+TPC], mixed[cc*128:(cc+1)*128, :], ch_ld[b])
        p.wait("vector", ch_ld[b], t_l)
        t_cast = p.step("vector", lambda e, _b=b: e.tensor_copy(mt[_b][:], mtb[_b][:]),
                        inc=ch_cv)
        cast_toks.append(t_cast)
        if cc >= 2:
            p.wait("vector", ch_sl, silu_toks[cc - 2])

        def conv(e, _b=b, _cc=cc):
            a = acc[_b]
            e.tensor_scalar(a[:], mt[_b][:, 0:TPC], cw[:, _cc*4:_cc*4+1], None,
                            op0=ALU.mult)
            last = None
            for j in (1, 2, 3):
                last = e.scalar_tensor_tensor(a[:], mt[_b][:, j:j+TPC],
                                              cw[:, _cc*4+j:_cc*4+j+1], a[:],
                                              op0=ALU.mult, op1=ALU.add)
            return last
        t_conv = p.step("vector", conv, inc=ch_cv)
        if DEBUG_LNQKV and cc < 4:
            p.wait("sync", ch_cv, t_conv)
            p.dma("sync", mt_dbg[cc], mt[b][:], p.chan("dbg"))
            p.dma("sync", acc_dbg[cc], acc[b][:], p.chan("dbg"))
            p.wait("vector", p.chan("dbg"), p.chan("dbg").n)
        p.wait("scalar", ch_cv, t_conv)
        if cc >= 2:
            p.wait("scalar", ch_st[b], 16 * (cc // 2))
        t_sl = p.step("scalar", lambda e, _b=b: e.activation(
            ob[_b][:], acc[_b][:], AF.Silu), inc=ch_sl)
        silu_toks.append(t_sl)
        p.wait("sync", ch_sl, t_sl)
        p.dma("sync", convm[cc*128:(cc+1)*128, :], ob[b][:], ch_st[b])
    p.wait("sync", ch_st[0], ch_st[0].n)
    p.wait("sync", ch_st[1], ch_st[1].n)
    return p.finalize()


def sub_dn_pre(x, w_qkv, w_z, w_b, w_a, conv_w, dt_bias, A_log):
    nc1 = build_dn_pre1()
    w_ba = np.concatenate([np.asarray(w_b), np.asarray(w_a)], axis=1)
    dtb = np.tile(np.asarray(dt_bias, np.float32)[None, :], (128, 1))
    alog = np.tile(np.asarray(A_log, np.float32)[None, :], (128, 1))
    im = [{"x": _f32(x[c*TPC:(c+1)*TPC]), "w_qkv": _f32(w_qkv), "w_z": _f32(w_z),
           "w_ba": _f32(w_ba), "dtb": _f32(dtb), "alog": _f32(alog),
           "ident": IDENT} for c in range(NCORES)]
    r1 = run(nc1, im).results
    mixed = [np.asarray(r["mixed"]) for r in r1]
    z = np.concatenate([np.asarray(r["z"]) for r in r1])
    g = np.concatenate([np.asarray(r["g"]) for r in r1])
    beta = np.concatenate([np.asarray(r["beta"]) for r in r1])

    nc2 = build_dn_pre2()
    zeros3 = np.zeros((8192, 3), dtype=BF)
    im2 = []
    for c in range(NCORES):
        if c % 4 == 0:
            h = zeros3
        else:
            h = np.ascontiguousarray(mixed[c - 1][:, -3:])
        im2.append({"mixed": mixed[c], "halo": h, "conv_w": _f32(conv_w)})
    r2 = run(nc2, im2).results
    convm = np.concatenate([np.asarray(r["convm"]) for r in r2], axis=1)
    return convm, z, g, beta



@lru_cache(maxsize=None)
def build_dn_scan():
    p = Prog("dn_scan")
    NP = 8
    NC = S // 128
    W8 = NP * 128
    q_in = p.dram_in("q_in", (NP, S, 128), BF16)
    k_in = p.dram_in("k_in", (NP, S, 128), BF16)
    v_in = p.dram_in("v_in", (NP, S, 128), BF16)
    g_in = p.dram_in("g_in", (128, NP * NC))
    beta_in = p.dram_in("beta_in", (128, NP * NC))
    utri_in = p.dram_in("utri", (128, 128))
    onesr_in = p.dram_in("onesr", (1, 128))
    onesrb_in = p.dram_in("onesr_bf", (1, 128), BF16)
    mask_rs_in = p.dram_in("mask_rs", (128, W8), BF16)
    mask_sr_in = p.dram_in("mask_sr", (128, W8), BF16)
    mask_incl_in = p.dram_in("mask_incl", (128, W8), BF16)
    identin = p.dram_in("ident", (128, 128), BF16)
    o_out = p.dram_out("o_out", (NP, S, 128), BF16)

    qkv_sb = [[p.sbuf(f"qkv{t}{j}", (128, W8), BF16) for j in range(2)]
              for t in range(3)]
    g_sb = p.sbuf("gsb", (128, NP * NC))
    b_sb = p.sbuf("bsb2", (128, NP * NC))
    utri = p.sbuf("utrisb", (128, 128))
    onesr = p.sbuf("onesrsb", (1, 128))
    onesrb = p.sbuf("onesrbsb", (1, 128), BF16)
    mrs = p.sbuf("mrs", (128, W8), BF16)
    msr = p.sbuf("msr", (128, W8), BF16)
    mincl = p.sbuf("minclsb", (128, W8), BF16)
    ident = p.sbuf("identsb", (128, 128), BF16)
    S32 = p.sbuf("S32", (128, W8))
    S_bf = p.sbuf("S_bf", (128, W8), BF16)
    qn = p.sbuf("qn", (128, W8), BF16)
    kn = p.sbuf("kn", (128, W8), BF16)
    qf = p.sbuf("qf", (128, W8), BF16)
    kf = p.sbuf("kf", (128, W8), BF16)
    ssq = p.sbuf("ssq", (128, 2 * NP))
    rst = p.sbuf("rst", (128, 2 * NP))
    g32c = p.sbuf("g32c", (128, NP))
    grow = p.sbuf("grow", (1, W8))
    bcr = p.sbuf("bcr", (1, NP))
    bcolb = p.sbuf("bcolb", (128, NP), BF16)
    bc = p.sbuf("bc", (128, NP))
    eg = p.sbuf("eg", (128, NP))
    dcol = p.sbuf("dcol", (128, NP))
    bb = p.sbuf("bb", (128, NP))
    esub = p.sbuf("esub", (128, W8))
    esA = p.sbuf("esA", (128, W8))
    Ers = p.sbuf("Ers", (128, W8), BF16)
    Esr = p.sbuf("Esr", (128, W8), BF16)
    brow = p.sbuf("brow", (128, W8), BF16)
    brow_r = p.sbuf("brow_r", (1, W8), BF16)
    mbrs = p.sbuf("mbrs", (128, W8), BF16)
    tA = p.sbuf("tA", (128, W8))
    P_sb = [p.sbuf(f"Pw{j}", (128, W8), BF16) for j in range(2)]
    PT_sb = [p.sbuf(f"PTw{j}", (128, W8), BF16) for j in range(2)]
    U_sb = p.sbuf("Usb", (128, W8), BF16)
    vs_sb = p.sbuf("vs", (128, W8), BF16)
    ks_sb = p.sbuf("ks", (128, W8), BF16)
    qb_sb = p.sbuf("qb", (128, W8), BF16)
    W_sb = p.sbuf("Wm", (128, W8), BF16)
    kd_sb = p.sbuf("kd", (128, W8), BF16)
    o_sb = [p.sbuf(f"osb{j}", (128, W8), BF16) for j in range(2)]

    pA = p.psum("pA", (128, W8))
    pB = p.psum("pB", (128, W8))
    pC = p.psum("pC", (128, W8))
    pT = p.psum("pT", (128, 1024), BF16)
    pT2 = p.psum("pT2", (128, 1024), BF16)

    ch_ld = [p.chan("ld0"), p.chan("ld1")]
    ch_tab = p.chan("tab")
    ch_t = p.chan("t"); ch_v = p.chan("v"); ch_s = p.chan("s")
    ch_st = [p.chan("st0"), p.chan("st1")]

    p.dma("sync", utri[:], utri_in, ch_tab)
    p.dma("sync", onesr[:], onesr_in, ch_tab)
    p.dma("sync", onesrb[:], onesrb_in, ch_tab)
    p.dma("sync", mrs[:], mask_rs_in, ch_tab)
    p.dma("sync", msr[:], mask_sr_in, ch_tab)
    p.dma("sync", mincl[:], mask_incl_in, ch_tab)
    p.dma("sync", ident[:], identin, ch_tab)
    p.dma("sync", g_sb[:], g_in, ch_tab)
    p.dma("sync", b_sb[:], beta_in, ch_tab)
    tab_all = ch_tab.n
    for eng in ("vector", "scalar", "tensor"):
        p.wait(eng, ch_tab, tab_all)

    p.step("vector", lambda e: e.memset(S32[:], 0.0))
    p.step("vector", lambda e: e.tensor_copy(S_bf[:], S32[:]), inc=ch_v)

    vs_toks = {}
    for ci in range(NC):
        par = ci % 2
        if ci >= 2:
            p.wait("gpsimd", ch_v, vs_toks[ci - 2])
        for t, srct in ((0, q_in), (1, k_in), (2, v_in)):
            p.dma("gpsimd", qkv_sb[t][par].ap().rearrange("p (n c) -> p n c", n=NP),
                  srct[:, ci*128:(ci+1)*128, :].rearrange("n p c -> p n c"),
                  ch_ld[par])
        t_ld = ch_ld[par].n
        qb_t, kb_t, vb_t = qkv_sb[0][par], qkv_sb[1][par], qkv_sb[2][par]

        p.wait("scalar", ch_ld[par], t_ld)
        if ci >= 1:
            p.wait("scalar", ch_v, n4_prev)
        t_sq1 = p.step("scalar", lambda e, _q=qb_t: e.activation(
            tA[:], _q[:], AF.Square), inc=ch_s)
        p.wait("vector", ch_s, t_sq1)
        t_r1 = p.step("vector", lambda e: e.reduce_sum(
            ssq.ap().rearrange("p (n c) -> p n c", c=1)[:, 0:NP, :],
            tA.ap().rearrange("p (n c) -> p n c", n=NP), axis=AX.X), inc=ch_v)
        p.wait("scalar", ch_v, t_r1)
        t_sq2 = p.step("scalar", lambda e, _k=kb_t: e.activation(
            tA[:], _k[:], AF.Square), inc=ch_s)
        p.wait("vector", ch_s, t_sq2)
        t_r2 = p.step("vector", lambda e: e.reduce_sum(
            ssq.ap().rearrange("p (n c) -> p n c", c=1)[:, NP:2*NP, :],
            tA.ap().rearrange("p (n c) -> p n c", n=NP), axis=AX.X), inc=ch_v)
        p.wait("vector", ch_v, t_r2)
        t_e1 = p.step("vector", lambda e: e.tensor_scalar_add(
            rst[:], ssq[:], 1e-6), inc=ch_v)
        p.wait("scalar", ch_v, t_e1)
        t_ln = p.step("scalar", lambda e: e.activation(rst[:], rst[:], AF.Ln), inc=ch_s)
        p.wait("scalar", ch_s, t_ln)
        t_ex = p.step("scalar", lambda e: e.activation(
            rst[:], rst[:], AF.Exp, scale=-0.5), inc=ch_s)
        p.wait("vector", ch_s, t_ex)

        def donorm(e, _q=qb_t, _k=kb_t):
            last = None
            for pr in range(NP):
                e.tensor_scalar(qn[:, pr*128:(pr+1)*128], _q[:, pr*128:(pr+1)*128],
                                rst[:, pr:pr+1], 128.0 ** -0.5,
                                op0=ALU.mult, op1=ALU.mult)
                last = e.tensor_scalar(kn[:, pr*128:(pr+1)*128],
                                       _k[:, pr*128:(pr+1)*128],
                                       rst[:, NP+pr:NP+pr+1], None, op0=ALU.mult)
            return last
        t_norm = p.step("vector", donorm, inc=ch_v)
        p.wait("tensor", ch_v, t_norm)

        for pr in range(NP):
            p.step("tensor", lambda e, _pr=pr: e.transpose(
                pT[:, _pr*128:(_pr+1)*128], qn[:, _pr*128:(_pr+1)*128], ident[:]),
                inc=ch_t if pr == NP - 1 else None)
        p.wait("vector", ch_t, ch_t.n)
        t_qf = p.step("vector", lambda e: e.tensor_copy(qf[:], pT[:]), inc=ch_v)
        p.wait("tensor", ch_v, t_qf)
        for pr in range(NP):
            p.step("tensor", lambda e, _pr=pr: e.transpose(
                pT2[:, _pr*128:(_pr+1)*128], kn[:, _pr*128:(_pr+1)*128], ident[:]),
                inc=ch_t if pr == NP - 1 else None)
        p.wait("vector", ch_t, ch_t.n)
        t_kf = p.step("vector", lambda e: e.tensor_copy(kf[:], pT2[:]), inc=ch_v)
        p.wait("tensor", ch_v, t_kf)

        for pr in range(NP):
            p.step("tensor", lambda e, _pr=pr, _ci=ci: e.matmul(
                pA[:, _pr:_pr+1], utri[:], g_sb[:, _pr*NC+_ci:_pr*NC+_ci+1],
                start=True, stop=True), inc=None)
        for pr in range(NP):
            p.step("tensor", lambda e, _pr=pr, _ci=ci: e.matmul(
                pB[0:1, _pr*128:(_pr+1)*128], g_sb[:, _pr*NC+_ci:_pr*NC+_ci+1],
                utri[:], start=True, stop=True),
                inc=ch_t if pr == NP - 1 else None)
        p.wait("vector", ch_t, ch_t.n)
        t_gc = p.step("vector", lambda e: e.tensor_copy(g32c[:], pA[:, 0:NP]), inc=ch_v)
        p.wait("vector", ch_v, t_gc)
        t_gr = p.step("vector", lambda e: e.tensor_copy(grow[0:1, :], pB[0:1, :]), inc=ch_v)
        p.wait("vector", ch_v, t_gr)
        t_bc1 = p.step("vector", lambda e: e.tensor_copy(
            bcr.ap().rearrange("a (n c) -> a n c", c=1),
            grow.ap().rearrange("a (n c) -> a n c", n=NP)[0:1, :, 127:128]), inc=ch_v)
        p.wait("tensor", ch_v, t_bc1)
        t_bcm = p.step("tensor", lambda e: e.matmul(
            pC[:, 0:NP], onesr[:], bcr[0:1, 0:NP], start=True, stop=True), inc=ch_t)
        p.wait("scalar", ch_t, t_bcm)
        t_bce = p.step("scalar", lambda e: e.activation(bc[:], pC[:, 0:NP], AF.Exp),
                       inc=ch_s)
        p.wait("scalar", ch_v, t_gc)
        t_eg = p.step("scalar", lambda e: e.activation(eg[:], g32c[:], AF.Exp), inc=ch_s)
        p.wait("vector", ch_t, t_bcm)
        t_gd = p.step("vector", lambda e: e.tensor_sub(dcol[:], pC[:, 0:NP], g32c[:]),
                      inc=ch_v)
        p.wait("scalar", ch_v, t_gd)
        t_dc = p.step("scalar", lambda e: e.activation(dcol[:], dcol[:], AF.Exp), inc=ch_s)

        p.wait("vector", ch_s, t_eg)

        def mkbb(e, _ci=ci):
            bcol = b_sb.ap().rearrange("p (n c) -> p n c", n=NP)[:, :, _ci:_ci+1]
            return e.tensor_mul(bb.ap().rearrange("p (n c) -> p n c", c=1),
                                bcol, eg.ap().rearrange("p (n c) -> p n c", c=1))
        t_bb = p.step("vector", mkbb, inc=ch_v)
        def mkbcolbf(e, _ci=ci):
            bcol = b_sb.ap().rearrange("p (n c) -> p n c", n=NP)[:, :, _ci:_ci+1]
            return e.tensor_copy(bcolb.ap().rearrange("p (n c) -> p n c", c=1), bcol)
        t_bcb = p.step("vector", mkbcolbf, inc=ch_v)

        p.wait("tensor", ch_s, t_bce)
        for hw in range(2):
            p.step("tensor", lambda e, _h=hw: e.matmul(
                pC[:, _h*512:(_h+1)*512], onesr[:], grow[0:1, _h*512:(_h+1)*512],
                start=True, stop=True), inc=ch_t if hw == 1 else None)
        p.wait("vector", ch_t, ch_t.n)

        def mkesub(e):
            last = None
            for pr in range(NP):
                last = e.tensor_scalar(esub[:, pr*128:(pr+1)*128],
                                       pC[:, pr*128:(pr+1)*128],
                                       g32c[:, pr:pr+1], None, op0=ALU.subtract)
            return last
        t_es = p.step("vector", mkesub, inc=ch_v)
        p.wait("vector", ch_v, t_es)
        t_cl1 = p.step("vector", lambda e: e.tensor_scalar_min(esA[:], esub[:], 0.0),
                       inc=ch_v)
        p.wait("scalar", ch_v, t_cl1)
        t_Ers = p.step("scalar", lambda e: e.activation(Ers[:], esA[:], AF.Exp), inc=ch_s)
        p.wait("vector", ch_s, t_Ers)
        t_cl2 = p.step("vector", lambda e: e.tensor_scalar_max(esA[:], esub[:], 0.0),
                       inc=ch_v)
        p.wait("scalar", ch_v, t_cl2)
        t_Esr = p.step("scalar", lambda e: e.activation(
            Esr[:], esA[:], AF.Exp, scale=-1.0), inc=ch_s)
        t_brow = p.step("scalar", lambda e: e.activation(brow[:], pC[:], AF.Exp), inc=ch_s)

        p.wait("tensor", ch_v, t_bcb)
        for pr in range(NP):
            p.step("tensor", lambda e, _pr=pr: e.transpose(
                pT[0:1, _pr*128:(_pr+1)*128], bcolb[:, _pr:_pr+1], ident[:]),
                inc=ch_t if pr == NP - 1 else None)
        p.wait("vector", ch_t, ch_t.n)
        t_br = p.step("vector", lambda e: e.tensor_copy(brow_r[0:1, :], pT[0:1, :]),
                      inc=ch_v)
        p.wait("tensor", ch_v, t_br)
        t_bbc = p.step("tensor", lambda e: e.matmul(
            pA[:, 0:512], onesrb[:], brow_r[0:1, 0:512], start=True, stop=True))
        t_bbc = p.step("tensor", lambda e: e.matmul(
            pA[:, 512:1024], onesrb[:], brow_r[0:1, 512:1024], start=True, stop=True),
            inc=ch_t)
        p.wait("vector", ch_t, t_bbc)
        t_mbrs = p.step("vector", lambda e: e.tensor_mul(mbrs[:], pA[:], mrs[:]),
                        inc=ch_v)

        p.wait("tensor", ch_s, t_brow)
        p.wait("tensor", ch_v, t_mbrs)
        for pr in range(NP):
            p.step("tensor", lambda e, _pr=pr: e.matmul(
                pA[:, _pr*128:(_pr+1)*128], kf[:, _pr*128:(_pr+1)*128],
                kf[:, _pr*128:(_pr+1)*128], start=True, stop=True), inc=None)
        for pr in range(NP):
            p.step("tensor", lambda e, _pr=pr: e.matmul(
                pB[:, _pr*128:(_pr+1)*128], kf[:, _pr*128:(_pr+1)*128],
                qf[:, _pr*128:(_pr+1)*128], start=True, stop=True), inc=None)
        for pr in range(NP):
            p.step("tensor", lambda e, _pr=pr: e.matmul(
                pC[:, _pr*128:(_pr+1)*128], kf[:, _pr*128:(_pr+1)*128],
                S_bf[:, _pr*128:(_pr+1)*128], start=True, stop=True),
                inc=ch_t if pr == NP - 1 else None)
        p.wait("vector", ch_t, ch_t.n)
        p.wait("vector", ch_s, t_Esr)

        t_w1 = p.step("vector", lambda e: e.tensor_mul(tA[:], pB[:], Ers[:]), inc=ch_v)
        p.wait("vector", ch_v, t_w1)
        t_w = p.step("vector", lambda e: e.tensor_mul(W_sb[:], tA[:], mincl[:]), inc=ch_v)

        t_n1 = p.step("vector", lambda e: e.tensor_mul(tA[:], pA[:], Ers[:]), inc=ch_v)
        p.wait("vector", ch_v, t_n1)
        t_n2 = p.step("vector", lambda e: e.tensor_mul(PT_sb[0][:], tA[:], mbrs[:]),
                      inc=ch_v)
        t_n3 = p.step("vector", lambda e: e.tensor_mul(tA[:], pA[:], Esr[:]), inc=ch_v)
        p.wait("vector", ch_v, t_n3)
        t_n4 = p.step("vector", lambda e: e.tensor_mul(P_sb[0][:], tA[:], msr[:]),
                      inc=ch_v)
        n4_prev = t_n4
        p.wait("vector", ch_v, t_n4)

        def mulbcol(e, _ci=ci):
            last = None
            for pr in range(NP):
                last = e.tensor_scalar(P_sb[0][:, pr*128:(pr+1)*128],
                                       P_sb[0][:, pr*128:(pr+1)*128],
                                       b_sb[:, pr*NC+_ci:pr*NC+_ci+1], None,
                                       op0=ALU.mult)
            return last
        t_n5 = p.step("vector", mulbcol, inc=ch_v)

        def mkvs(e, _v=vb_t, _ci=ci):
            last = None
            for pr in range(NP):
                last = e.tensor_scalar(vs_sb[:, pr*128:(pr+1)*128],
                                       _v[:, pr*128:(pr+1)*128],
                                       b_sb[:, pr*NC+_ci:pr*NC+_ci+1], None,
                                       op0=ALU.mult)
            return last
        t_vs = p.step("vector", mkvs, inc=ch_v)
        vs_toks[ci] = t_vs
        p.wait("vector", ch_v, t_bb)

        def mkks(e):
            last = None
            for pr in range(NP):
                last = e.tensor_scalar(ks_sb[:, pr*128:(pr+1)*128],
                                       pC[:, pr*128:(pr+1)*128],
                                       bb[:, pr:pr+1], None, op0=ALU.mult)
            return last
        t_ks = p.step("vector", mkks, inc=ch_v)
        p.wait("vector", ch_v, t_ks)
        t_u0 = p.step("vector", lambda e: e.tensor_sub(U_sb[:], vs_sb[:], ks_sb[:]),
                      inc=ch_v)

        cur = 0
        t_gate = t_u0
        for j in range(7):
            p.wait("tensor", ch_v, t_gate)
            for pr in range(NP):
                p.step("tensor", lambda e, _pr=pr, _c=cur: e.matmul(
                    pA[:, _pr*128:(_pr+1)*128], PT_sb[_c][:, _pr*128:(_pr+1)*128],
                    U_sb[:, _pr*128:(_pr+1)*128], start=True, stop=True),
                    inc=ch_t if pr == NP - 1 else None)
            t_ap = ch_t.n
            if j < 6:
                for pr in range(NP):
                    p.step("tensor", lambda e, _pr=pr, _c=cur: e.matmul(
                        pB[:, _pr*128:(_pr+1)*128], PT_sb[_c][:, _pr*128:(_pr+1)*128],
                        P_sb[_c][:, _pr*128:(_pr+1)*128], start=True, stop=True),
                        inc=None)
                for pr in range(NP):
                    p.step("tensor", lambda e, _pr=pr, _c=cur: e.matmul(
                        pC[:, _pr*128:(_pr+1)*128], P_sb[_c][:, _pr*128:(_pr+1)*128],
                        PT_sb[_c][:, _pr*128:(_pr+1)*128], start=True, stop=True),
                        inc=ch_t if pr == NP - 1 else None)
            t_sq = ch_t.n
            p.wait("vector", ch_t, t_ap)
            t_unew = p.step("vector", lambda e: e.tensor_add(U_sb[:], pA[:], U_sb[:]),
                            inc=ch_v)
            if j < 6:
                p.wait("vector", ch_t, t_sq)
                t_p1 = p.step("vector", lambda e, _c=cur: e.tensor_copy(
                    P_sb[1 - _c][:], pB[:]), inc=ch_v)
                t_p2 = p.step("vector", lambda e, _c=cur: e.tensor_copy(
                    PT_sb[1 - _c][:], pC[:]), inc=ch_v)
                t_gate = t_p2
                cur = 1 - cur
            else:
                t_gate = t_unew

        p.wait("vector", ch_s, ch_s.n)
        t_qb = p.step("vector", lambda e: e.tensor_mul(qb_sb[:], qf[:], brow[:]),
                      inc=ch_v)
        p.wait("tensor", ch_v, t_gate)
        p.wait("tensor", ch_v, t_qb)
        for pr in range(NP):
            p.step("tensor", lambda e, _pr=pr: e.matmul(
                pA[:, _pr*128:(_pr+1)*128], qb_sb[:, _pr*128:(_pr+1)*128],
                S_bf[:, _pr*128:(_pr+1)*128], start=True, stop=False), inc=None)
            p.step("tensor", lambda e, _pr=pr: e.matmul(
                pA[:, _pr*128:(_pr+1)*128], W_sb[:, _pr*128:(_pr+1)*128],
                U_sb[:, _pr*128:(_pr+1)*128], start=False, stop=True),
                inc=ch_t if pr == NP - 1 else None)
        t_omm = ch_t.n

        p.wait("vector", ch_s, t_dc)
        def mkkd(e):
            last = None
            for pr in range(NP):
                last = e.tensor_scalar(kd_sb[:, pr*128:(pr+1)*128],
                                       kn[:, pr*128:(pr+1)*128],
                                       dcol[:, pr:pr+1], None, op0=ALU.mult)
            return last
        t_kd = p.step("vector", mkkd, inc=ch_v)
        p.wait("tensor", ch_v, t_kd)
        for pr in range(NP):
            p.step("tensor", lambda e, _pr=pr: e.matmul(
                pB[:, _pr*128:(_pr+1)*128], kd_sb[:, _pr*128:(_pr+1)*128],
                U_sb[:, _pr*128:(_pr+1)*128], start=True, stop=True),
                inc=ch_t if pr == NP - 1 else None)
        t_smm = ch_t.n

        p.wait("vector", ch_t, t_omm)
        ob = o_sb[par]
        if ci >= 2:
            p.wait("vector", ch_st[par], 16 * (ci // 2))
        t_oev = p.step("vector", lambda e, _ob=ob: e.tensor_copy(_ob[:], pA[:]),
                       inc=ch_v)
        p.wait("sync", ch_v, t_oev)
        p.dma("sync",
              o_out[:, ci*128:(ci+1)*128, :].rearrange("n p c -> p n c"),
              ob.ap().rearrange("p (n c) -> p n c", n=NP), ch_st[par])

        p.wait("vector", ch_t, t_smm)
        p.wait("scalar", ch_v, 0)

        def supd(e):
            last = None
            for pr in range(NP):
                last = e.scalar_tensor_tensor(
                    S32[:, pr*128:(pr+1)*128], S32[:, pr*128:(pr+1)*128],
                    bc[:, pr:pr+1], pB[:, pr*128:(pr+1)*128],
                    op0=ALU.mult, op1=ALU.add)
            return last
        t_su = p.step("vector", supd, inc=ch_v)
        p.wait("vector", ch_v, t_su)
        t_sbf = p.step("vector", lambda e: e.tensor_copy(S_bf[:], S32[:]), inc=ch_v)
    p.wait("sync", ch_st[0], ch_st[0].n)
    p.wait("sync", ch_st[1], ch_st[1].n)
    return p.finalize()


def sub_dn_scan(convm, g, beta):
    nc = build_dn_scan()
    NP, NC = 8, 32
    utri = np.triu(np.ones((128, 128), dtype=np.float32))
    onesr = np.ones((1, 128), np.float32)
    onesrb = np.ones((1, 128), BF)
    r = np.arange(128)
    m_rs = np.where(r[:, None] < r[None, :], -1.0, 0.0).astype(BF)
    m_sr = np.where(r[:, None] > r[None, :], -1.0, 0.0).astype(BF)
    m_in = np.where(r[:, None] <= r[None, :], 1.0, 0.0).astype(BF)
    tile8 = lambda m: np.ascontiguousarray(np.tile(m, (1, NP)))
    im = []
    for c in range(NCORES):
        qs, ks, vs, gs, bs = [], [], [], [], []
        for pi in range(8):
            pr = c * 8 + pi
            b, vh = pr // 32, pr % 32
            kh = vh // 2
            cols = slice(b * S, (b + 1) * S)
            qs.append(np.ascontiguousarray(convm[128*kh:128*(kh+1), cols].T))
            ks.append(np.ascontiguousarray(convm[2048+128*kh:2048+128*(kh+1), cols].T))
            vs.append(np.ascontiguousarray(convm[4096+128*vh:4096+128*(vh+1), cols].T))
            gs.append(np.asarray(g[b*S:(b+1)*S, vh], np.float32).reshape(NC, 128))
            bs.append(np.asarray(beta[b*S:(b+1)*S, vh], np.float32).reshape(NC, 128))
        im.append({
            "q_in": np.stack(qs), "k_in": np.stack(ks), "v_in": np.stack(vs),
            "g_in": np.ascontiguousarray(np.stack(gs).reshape(NP * NC, 128).T),
            "beta_in": np.ascontiguousarray(np.stack(bs).reshape(NP * NC, 128).T),
            "utri": utri, "onesr": onesr, "onesr_bf": onesrb,
            "mask_rs": tile8(m_rs), "mask_sr": tile8(m_sr),
            "mask_incl": tile8(m_in), "ident": IDENT,
        })
    rs = run(nc, im).results
    o = np.concatenate([np.asarray(r_["o_out"]) for r_ in rs])
    o = o.reshape(B, 32, S, 128).transpose(0, 2, 1, 3).reshape(T, 4096)
    return o



@lru_cache(maxsize=None)
def build_dn_gate():
    p = Prog("dn_gate")
    VD = 4096
    o_in = p.dram_in("o", (TPC, VD), BF16)
    z_in = p.dram_in("z", (TPC, VD), BF16)
    gam = p.dram_in("gamma", (128, VD))
    on_out = p.dram_out("on", (TPC, VD), BF16)

    ot = [p.sbuf(f"ot{j}", (128, VD)) for j in range(2)]
    zt = [p.sbuf(f"zt{j}", (128, VD)) for j in range(2)]
    gt = p.sbuf("gt", (128, VD))
    og = p.sbuf("og", (128, VD))
    sq = p.sbuf("sq", (128, VD))
    ss = p.sbuf("ssx", (128, 32))
    onb = [p.sbuf(f"onb{j}", (128, VD), BF16) for j in range(2)]
    ch_ld = [p.chan("ld0"), p.chan("ld1")]
    ch_g = p.chan("g"); ch_v = p.chan("v"); ch_s = p.chan("s")
    ch_st = [p.chan("st0"), p.chan("st1")]

    p.dma("sync", gt[:], gam, ch_g)
    p.wait("vector", ch_g, 16)
    don_toks = []
    for i in range(NT):
        b = i % 2
        if i >= 2:
            p.wait("gpsimd", ch_v, don_toks[i - 2])
        p.dma("gpsimd", ot[b][:], o_in[i*128:(i+1)*128, :], ch_ld[b])
        t_l = p.dma("gpsimd", zt[b][:], z_in[i*128:(i+1)*128, :], ch_ld[b])
        p.wait("scalar", ch_ld[b], t_l)
        t_zs = p.step("scalar", lambda e, _b=b: e.activation(
            zt[_b][:], zt[_b][:], AF.Silu), inc=ch_s)
        p.wait("vector", ch_s, t_zs)
        t_og = p.step("vector", lambda e, _b=b: e.tensor_mul(og[:], ot[_b][:], zt[_b][:]),
                      inc=ch_v)
        p.wait("scalar", ch_v, t_og)
        t_sq = p.step("scalar", lambda e: e.activation(sq[:], og[:], AF.Square), inc=ch_s)
        p.wait("vector", ch_s, t_sq)
        t_r = p.step("vector", lambda e: e.reduce_sum(
            ss.ap().rearrange("p (n c) -> p n c", c=1),
            sq.ap().rearrange("p (n c) -> p n c", n=32), axis=AX.X), inc=ch_v)
        p.wait("vector", ch_v, t_r)
        t_m = p.step("vector", lambda e: e.tensor_scalar(
            ss[:], ss[:], 1.0 / 128.0, 1e-6, op0=ALU.mult, op1=ALU.add), inc=ch_v)
        p.wait("scalar", ch_v, t_m)
        t_ln = p.step("scalar", lambda e: e.activation(ss[:], ss[:], AF.Ln), inc=ch_s)
        p.wait("scalar", ch_s, t_ln)
        t_ex = p.step("scalar", lambda e: e.activation(ss[:], ss[:], AF.Exp, scale=-0.5),
                      inc=ch_s)
        p.wait("vector", ch_s, t_ex)

        def appl(e):
            last = None
            for h in range(32):
                last = e.tensor_scalar(og[:, h*128:(h+1)*128], og[:, h*128:(h+1)*128],
                                       ss[:, h:h+1], None, op0=ALU.mult)
            return last
        t_ap = p.step("vector", appl, inc=ch_v)
        p.wait("vector", ch_v, t_ap)
        if i >= 2:
            p.wait("vector", ch_st[b], 16 * (i // 2))
        t_don = p.step("vector", lambda e, _b=b: e.tensor_mul(onb[_b][:], og[:], gt[:]),
                       inc=ch_v)
        don_toks.append(t_don)
        p.wait("sync", ch_v, t_don)
        p.dma("sync", on_out[i*128:(i+1)*128, :], onb[b][:], ch_st[b])
    p.wait("sync", ch_st[0], ch_st[0].n)
    p.wait("sync", ch_st[1], ch_st[1].n)
    return p.finalize()


def sub_attn_b_L0(x, inp):
    convm, z, g, beta = sub_dn_pre(
        x, inp["L0_attn_b_w_qkv"], inp["L0_attn_b_w_z"], inp["L0_attn_b_w_b"],
        inp["L0_attn_b_w_a"], inp["L0_attn_b_conv_w"], inp["L0_attn_b_dt_bias"],
        inp["L0_attn_b_A_log"])
    o = sub_dn_scan(convm, g, beta)
    nc_g = build_dn_gate()
    gam = np.tile(np.asarray(inp["L0_attn_b_norm_gamma"], np.float32)[None, :],
                  (128, 32))
    im = [{"o": np.ascontiguousarray(o[c*TPC:(c+1)*TPC]),
           "z": np.ascontiguousarray(z[c*TPC:(c+1)*TPC]), "gamma": gam}
          for c in range(NCORES)]
    rg = run(nc_g, im).results
    on = np.concatenate([np.asarray(r["on"]) for r in rg])
    nc_p = build_projres(4096, False)
    im2 = [{"af": np.ascontiguousarray(on[c*TPC:(c+1)*TPC].T),
            "w": _f32(inp["L0_attn_b_w_out"]), "x": _f32(x[c*TPC:(c+1)*TPC])}
           for c in range(NCORES)]
    rp = run(nc_p, im2).results
    return np.concatenate([np.asarray(r["xo"]) for r in rp])



@lru_cache(maxsize=None)
def build_moe():
    p = Prog("moe")
    E, F = 8, 1024
    x = p.dram_in("x", (TPC, D))
    rw = p.dram_in("rw", (D, E))
    gate_w = p.dram_in("gate_w", (E, D, F))
    up_w = p.dram_in("up_w", (E, D, F))
    down_w = p.dram_in("down_w", (E, F, D))
    identin = p.dram_in("ident", (128, 128), BF16)
    xo = p.dram_out("xo", (TPC, D))

    acc = [p.sbuf(f"acc{i}", (128, D)) for i in range(NT)]
    xn_sb = [p.sbuf(f"xn{j}", (128, D), BF16) for j in range(2)]
    xnf_sb = [p.sbuf(f"xnf{d}", (128, TPC), BF16) for d in range(DCH)]
    ident = p.sbuf("identsb", (128, 128), BF16)
    rw_sb = p.sbuf("rwsb", (128, DCH * E), BF16)
    gw_sb = p.sbuf("gwsb", (128, DCH * 512), BF16)
    uw_sb = p.sbuf("uwsb", (128, DCH * 512), BF16)
    dw_sb = p.sbuf("dwsb", (128, 4 * D), BF16)
    lg = p.sbuf("lg", (128, 8))
    eq1 = p.sbuf("eq1", (128, 8))
    eq2 = p.sbuf("eq2", (128, 8))
    msk = p.sbuf("msk", (128, 8))
    l1 = p.sbuf("l1", (128, 1))
    l2 = p.sbuf("l2", (128, 1))
    w1 = p.sbuf("w1s", (128, 1))
    w2 = p.sbuf("w2s", (128, 1))
    cw = p.sbuf("cw", (128, NT * E))
    h1 = p.sbuf("h1m", (128, 512))
    hbf = p.sbuf("hbfm", (128, 512), BF16)
    hfm = p.sbuf("hfmm", (128, 512), BF16)
    scratch = ln_scratch(p)
    pG = p.psum("pG", (128, 512))
    pU = p.psum("pU", (128, 512))
    pTa = p.psum("pTa", (128, 1024), BF16)
    pTb = p.psum("pTb", (128, 1024), BF16)
    pD = [p.psum(f"pD{j}", (128, 512)) for j in range(4)]

    ch_x = p.chan("chx"); ch_tab = p.chan("tab"); ch_w = p.chan("w")
    ch_ln = p.chan("ln"); ch_tr = p.chan("tr"); ch_trev = p.chan("trev")
    ch_r = p.chan("r"); ch_rt = p.chan("rt"); ch_rs = p.chan("rs")
    ch_mx = p.chan("mx"); ch_h = p.chan("h"); ch_he = p.chan("he")
    ch_mo = p.chan("mo"); ch_ev = p.chan("ev")
    ch_st = [p.chan("sto0"), p.chan("sto1")]

    p.dma("sync", ident[:], identin, ch_tab)
    for i in range(NT):
        p.dma("sync", acc[i][:], x[i*128:(i+1)*128, :], ch_x)
    t_rw = p.dma("gpsimd", rw_sb.ap().rearrange("p (d c) -> p d c", d=DCH),
                 rw.rearrange("(d p) o -> p d o", p=128), ch_w)
    x_all = ch_x.n
    for eng in ("vector", "scalar"):
        p.wait(eng, ch_x, x_all)
    p.wait("tensor", ch_tab, ch_tab.n)
    p.wait("tensor", ch_w, t_rw)

    tr_cnt = 0
    ln_toks = []
    for i in range(NT):
        nb = xn_sb[i % 2]
        if i >= 2:
            p.wait("scalar", ch_tr, (i - 1) * DCH)
        t_ln = walk_layernorm_tile(p, acc[i][:], nb[:], scratch, i, ch_ln)
        ln_toks.append(t_ln)
        p.wait("tensor", ch_ln, t_ln)
        for d in range(DCH):
            pT = pTa if tr_cnt % 2 == 0 else pTb
            if tr_cnt >= 2:
                p.wait("tensor", ch_trev, tr_cnt - 1)
            t_tr = p.step("tensor", lambda e, _n=nb, _d=d, _p=pT: e.transpose(
                _p[:, 0:128], _n[:, _d*128:(_d+1)*128], ident[:]), inc=ch_tr)
            p.wait("vector", ch_tr, t_tr)
            p.step("vector", lambda e, _i=i, _d=d, _p=pT: e.tensor_copy(
                xnf_sb[_d][:, _i*128:(_i+1)*128], _p[:, 0:128]), inc=ch_trev)
            tr_cnt += 1
        p.wait("tensor", ch_trev, ch_trev.n)
        if i >= 1:
            p.wait("tensor", ch_r, i)
        for d in range(DCH):
            p.step("tensor", lambda e, _d=d, _i=i, _s=(d == 0), _e=(d == DCH - 1):
                   e.matmul(pG[:, 0:8], xnf_sb[_d][:, _i*128:(_i+1)*128],
                            rw_sb[:, _d*8:(_d+1)*8], start=_s, stop=_e),
                   inc=ch_rt if d == DCH - 1 else None)
        p.wait("vector", ch_rt, ch_rt.n)
        t_lg = p.step("vector", lambda e: e.tensor_copy(lg[:], pG[:, 0:8]), inc=ch_r)
        p.wait("vector", ch_r, t_lg)
        t_l1 = p.step("vector", lambda e: e.reduce_max(l1[:], lg[:], axis=AX.X), inc=ch_rs)
        p.wait("vector", ch_rs, t_l1)
        t_e1 = p.step("vector", lambda e: e.tensor_scalar(
            eq1[:], lg[:], l1[:, 0:1], None, op0=ALU.is_equal), inc=ch_rs)
        p.wait("vector", ch_rs, t_e1)
        t_mk = p.step("vector", lambda e: e.scalar_tensor_tensor(
            msk[:], eq1[:], -1e30, lg[:], op0=ALU.mult, op1=ALU.add), inc=ch_rs)
        p.wait("vector", ch_rs, t_mk)
        t_l2 = p.step("vector", lambda e: e.reduce_max(l2[:], msk[:], axis=AX.X), inc=ch_rs)
        p.wait("vector", ch_rs, t_l2)
        t_e2 = p.step("vector", lambda e: e.tensor_scalar(
            eq2[:], msk[:], l2[:, 0:1], None, op0=ALU.is_equal), inc=ch_rs)
        p.wait("vector", ch_rs, t_e2)
        t_d = p.step("vector", lambda e: e.tensor_sub(w1[:], l1[:], l2[:]), inc=ch_rs)
        p.wait("scalar", ch_rs, t_d)
        t_w1 = p.step("scalar", lambda e: e.activation(w1[:], w1[:], AF.Sigmoid), inc=ch_rs)
        p.wait("vector", ch_rs, t_w1)
        t_w2 = p.step("vector", lambda e: e.tensor_scalar(
            w2[:], w1[:], -1.0, 1.0, op0=ALU.mult, op1=ALU.add), inc=ch_rs)
        p.wait("vector", ch_rs, t_w2)
        t_c1 = p.step("vector", lambda e, _i=i: e.tensor_scalar(
            cw[:, _i*E:(_i+1)*E], eq1[:], w1[:, 0:1], None, op0=ALU.mult), inc=ch_rs)
        p.wait("vector", ch_rs, t_c1)
        t_c2 = p.step("vector", lambda e, _i=i: e.scalar_tensor_tensor(
            cw[:, _i*E:(_i+1)*E], eq2[:], w2[:, 0:1], cw[:, _i*E:(_i+1)*E],
            op0=ALU.mult, op1=ALU.add), inc=ch_rs)
    cw_done = ch_rs.n
    p.wait("vector", ch_rs, cw_done)
    p.wait("tensor", ch_r, NT)

    gv = gate_w.rearrange("e (d p) f -> e p d f", p=128)
    uv = up_w.rearrange("e (d p) f -> e p d f", p=128)
    dv = down_w.rearrange("e (f p) o -> e p f o", p=128)
    idx = 0
    for e_i in range(E):
        for fh in range(2):
            p.wait("gpsimd", ch_mo, idx * 4)
            p.dma("gpsimd", gw_sb.ap().rearrange("p (d c) -> p d c", d=DCH),
                  gv[e_i][:, :, fh*512:(fh+1)*512], ch_w)
            p.dma("gpsimd", uw_sb.ap().rearrange("p (d c) -> p d c", d=DCH),
                  uv[e_i][:, :, fh*512:(fh+1)*512], ch_w)
            t_w = p.dma("gpsimd", dw_sb.ap().rearrange("p (h o) -> p h o", h=4),
                        dv[e_i][:, fh*4:(fh+1)*4, :], ch_w)
            p.wait("tensor", ch_w, t_w)
            for i in range(NT):
                if idx >= 1:
                    p.wait("tensor", ch_h, 2 * idx)
                for d in range(DCH):
                    p.step("tensor", lambda e, _d=d, _i=i, _s=(d == 0), _e=(d == DCH - 1):
                           e.matmul(pG[:], xnf_sb[_d][:, _i*128:(_i+1)*128],
                                    gw_sb[:, _d*512:(_d+1)*512], start=_s, stop=_e),
                           inc=ch_mx if d == DCH - 1 else None)
                for d in range(DCH):
                    p.step("tensor", lambda e, _d=d, _i=i, _s=(d == 0), _e=(d == DCH - 1):
                           e.matmul(pU[:], xnf_sb[_d][:, _i*128:(_i+1)*128],
                                    uw_sb[:, _d*512:(_d+1)*512], start=_s, stop=_e),
                           inc=ch_mx if d == DCH - 1 else None)
                t_mx = ch_mx.n
                p.wait("scalar", ch_mx, t_mx - 1)
                t_sig = p.step("scalar", lambda e: e.activation(h1[:], pG[:], AF.Silu),
                               inc=ch_h)
                p.wait("vector", ch_mx, t_mx)
                p.wait("vector", ch_h, t_sig)
                t_h = p.step("vector", lambda e: e.tensor_mul(hbf[:], pU[:], h1[:]),
                             inc=ch_h)
                p.wait("tensor", ch_h, 2 * idx + 2)
                if idx >= 1:
                    p.wait("tensor", ch_he, idx)
                for j in range(4):
                    p.step("tensor", lambda e, _j=j: e.transpose(
                        pTa[:, _j*128:(_j+1)*128], hbf[:, _j*128:(_j+1)*128],
                        ident[:]), inc=ch_tr if j == 3 else None)
                p.wait("vector", ch_tr, ch_tr.n)
                t_he = p.step("vector", lambda e: e.tensor_copy(hfm[:], pTa[:, 0:512]),
                              inc=ch_he)
                p.wait("tensor", ch_he, t_he)
                for osub in range(4):
                    if idx >= 1:
                        p.wait("tensor", ch_ev, (idx - 1) * 4 + osub + 1)
                    for hsub in range(4):
                        p.step("tensor", lambda e, _h=hsub, _o=osub,
                               _s=(hsub == 0), _e=(hsub == 3):
                               e.matmul(pD[_o][:], hfm[:, _h*128:(_h+1)*128],
                                        dw_sb[:, _h*D + _o*512:_h*D + (_o+1)*512],
                                        start=_s, stop=_e),
                               inc=ch_mo if hsub == 3 else None)
                mo_base = ch_mo.n
                for osub in range(4):
                    p.wait("vector", ch_mo, mo_base - 3 + osub)
                    p.step("vector", lambda e, _i=i, _o=osub, _e=e_i:
                           e.scalar_tensor_tensor(
                               acc[_i][:, _o*512:(_o+1)*512], pD[_o][:],
                               cw[:, _i*E+_e:_i*E+_e+1],
                               acc[_i][:, _o*512:(_o+1)*512],
                               op0=ALU.mult, op1=ALU.add), inc=ch_ev)
                idx += 1
    p.wait("sync", ch_ev, ch_ev.n)
    for i in range(NT):
        p.dma("sync", xo[i*128:(i+1)*128, :], acc[i][:], ch_st[i % 2])
    p.wait("sync", ch_st[0], ch_st[0].n)
    p.wait("sync", ch_st[1], ch_st[1].n)
    return p.finalize()


def sub_moe(x, inp):
    nc = build_moe()
    im = [{"x": _f32(x[c*TPC:(c+1)*TPC]), "rw": _f32(inp["L1_ffn_router_w"]),
           "gate_w": _f32(inp["L1_ffn_gate_proj"]), "up_w": _f32(inp["L1_ffn_up_proj"]),
           "down_w": _f32(inp["L1_ffn_down_proj"]), "ident": IDENT}
          for c in range(NCORES)]
    rs = run(nc, im).results
    return np.concatenate([np.asarray(r["xo"]) for r in rs])



@lru_cache(maxsize=None)
def build_gqa_pre():
    p = Prog("gqa_pre")
    OW = 3072
    x = p.dram_in("x", (TPC, D))
    wcat = p.dram_in("wcat", (D, OW))
    acat = p.dram_in("acat", (D, 48))
    bcat = p.dram_in("bcat", (48, OW))
    gain = p.dram_in("gain", (128, 16))
    cosr = p.dram_in("cosr", (TPC, 32))
    sinr = p.dram_in("sinr", (TPC, 32))
    identin = p.dram_in("ident", (128, 128), BF16)
    q_o = p.dram_out("q", (TPC, D), BF16)
    k_o = p.dram_out("k", (TPC, 512), BF16)
    v_o = p.dram_out("v", (TPC, 512), BF16)

    x_sb = [p.sbuf(f"x{j}", (128, D)) for j in range(2)]
    xn_sb = [p.sbuf(f"xn{j}", (128, D), BF16) for j in range(2)]
    xnf_sb = [p.sbuf(f"xnf{d}", (128, TPC), BF16) for d in range(DCH)]
    ident = p.sbuf("identsb", (128, 128), BF16)
    w_sb = p.sbuf("wsb", (128, DCH * OW), BF16)
    a_sb = p.sbuf("asb", (128, DCH * 48), BF16)
    b_sb = p.sbuf("bsb3", (48, OW), BF16)
    midf = p.sbuf("midf", (48, TPC), BF16)
    gn_sb = p.sbuf("gnsb", (128, 16))
    cos_sb = [p.sbuf(f"cos{i}", (128, 32)) for i in range(NT)]
    sin_sb = [p.sbuf(f"sin{i}", (128, 32)) for i in range(NT)]
    o32 = [p.sbuf(f"o32{j}", (128, OW)) for j in range(2)]
    sq = p.sbuf("sqw", (128, D))
    ss = p.sbuf("ssg", (128, 20))
    rr = p.sbuf("rrg", (128, 20))
    t0 = p.sbuf("t0g", (128, 32))
    t1 = p.sbuf("t1g", (128, 32))
    t2 = p.sbuf("t2g", (128, 32))
    t3 = p.sbuf("t3g", (128, 32))
    scratch = ln_scratch(p)
    pTa = p.psum("pTa", (128, 1024), BF16)
    pTb = p.psum("pTb", (128, 1024), BF16)
    pM = p.psum("pMg", (48, TPC))
    pO = [p.psum(f"pOg{j}", (128, 512)) for j in range(4)]

    ch_x = [p.chan("x0"), p.chan("x1")]
    ch_tab = p.chan("tab"); ch_w = p.chan("w"); ch_ln = p.chan("ln")
    ch_tr = p.chan("tr"); ch_trev = p.chan("trev")
    ch_mid = p.chan("mid"); ch_mm = p.chan("mm"); ch_ev = p.chan("ev")
    ch_n = p.chan("n"); ch_ns = p.chan("ns")
    ch_st = [p.chan("st0"), p.chan("st1")]

    p.dma("sync", ident[:], identin, ch_tab)
    p.dma("sync", gn_sb[:], gain, ch_tab)
    for i in range(NT):
        p.dma("sync", cos_sb[i][:], cosr[i*128:(i+1)*128, :], ch_tab)
        p.dma("sync", sin_sb[i][:], sinr[i*128:(i+1)*128, :], ch_tab)
    tab_all = ch_tab.n
    t_w = p.dma("gpsimd", w_sb.ap().rearrange("p (d c) -> p d c", d=DCH),
                wcat.rearrange("(d p) o -> p d o", p=128), ch_w)
    p.dma("gpsimd", a_sb.ap().rearrange("p (d c) -> p d c", d=DCH),
          acat.rearrange("(d p) o -> p d o", p=128), ch_w)
    t_wl = p.dma("gpsimd", b_sb[:], bcat, ch_w)
    for eng in ("vector", "scalar", "tensor"):
        p.wait(eng, ch_tab, tab_all)

    tr_cnt = 0
    ln_toks = []
    for i in range(NT):
        xb, nb = x_sb[i % 2], xn_sb[i % 2]
        if i >= 2:
            p.wait("sync", ch_ln, ln_toks[i - 2])
        t_x = p.dma("sync", xb[:], x[i*128:(i+1)*128, :], ch_x[i % 2])
        p.wait("vector", ch_x[i % 2], t_x)
        p.wait("scalar", ch_x[i % 2], t_x)
        if i >= 2:
            p.wait("scalar", ch_tr, (i - 1) * DCH)
        t_ln = walk_layernorm_tile(p, xb[:], nb[:], scratch, i, ch_ln)
        ln_toks.append(t_ln)
        p.wait("tensor", ch_ln, t_ln)
        for d in range(DCH):
            pT = pTa if tr_cnt % 2 == 0 else pTb
            if tr_cnt >= 2:
                p.wait("tensor", ch_trev, tr_cnt - 1)
            t_tr = p.step("tensor", lambda e, _n=nb, _d=d, _p=pT: e.transpose(
                _p[:, 0:128], _n[:, _d*128:(_d+1)*128], ident[:]), inc=ch_tr)
            p.wait("vector", ch_tr, t_tr)
            p.step("vector", lambda e, _i=i, _d=d, _p=pT: e.tensor_copy(
                xnf_sb[_d][:, _i*128:(_i+1)*128], _p[:, 0:128]), inc=ch_trev)
            tr_cnt += 1
    tr_all = ch_trev.n
    p.wait("tensor", ch_trev, tr_all)
    p.wait("tensor", ch_w, t_wl)

    for hh in range(2):
        for d in range(DCH):
            p.step("tensor", lambda e, _d=d, _h=hh, _s=(d == 0), _e=(d == DCH - 1):
                   e.matmul(pM[:, _h*512:(_h+1)*512], a_sb[:, _d*48:(_d+1)*48],
                            xnf_sb[_d][:, _h*512:(_h+1)*512],
                            start=_s, stop=_e),
                   inc=ch_mid if (d == DCH - 1 and hh == 1) else None)
    p.wait("vector", ch_mid, 1)
    t_mev = p.step("vector", lambda e: e.tensor_copy(midf[:], pM[:]), inc=ch_mid)
    p.wait("tensor", ch_mid, t_mev)

    cnt = 0
    for i in range(NT):
        for oc in range(6):
            bank = cnt % 4
            if cnt >= 4:
                p.wait("tensor", ch_ev, cnt - 3)
            for d in range(DCH):
                p.step("tensor", lambda e, _d=d, _i=i, _oc=oc, _b=bank, _s=(d == 0):
                       e.matmul(pO[_b][:], xnf_sb[_d][:, _i*128:(_i+1)*128],
                                w_sb[:, _d*OW + _oc*512:_d*OW + (_oc+1)*512],
                                start=_s, stop=False), inc=None)
            p.step("tensor", lambda e, _i=i, _oc=oc, _b=bank:
                   e.matmul(pO[_b][:], midf[:, _i*128:(_i+1)*128],
                            b_sb[:, _oc*512:(_oc+1)*512],
                            start=False, stop=True), inc=ch_mm)
            cnt += 1
            p.wait("vector", ch_mm, cnt)
            if i >= 2 and oc == 0:
                p.wait("vector", ch_st[i % 2], 48 * (i // 2))
            p.step("vector", lambda e, _i=i, _oc=oc, _b=bank: e.tensor_copy(
                o32[_i % 2][:, _oc*512:(_oc+1)*512], pO[_b][:]), inc=ch_ev)
        p.wait("scalar", ch_ev, ch_ev.n)
        ot = o32[i % 2]
        t_sq = p.step("scalar", lambda e, _o=ot: e.activation(
            sq[:], _o[:, 0:D], AF.Square), inc=ch_ns)
        p.wait("vector", ch_ns, t_sq)
        t_r1 = p.step("vector", lambda e: e.reduce_sum(
            ss.ap().rearrange("p (n c) -> p n c", c=1)[:, 0:16, :],
            sq.ap().rearrange("p (n c) -> p n c", n=16), axis=AX.X), inc=ch_n)
        p.wait("scalar", ch_n, t_r1)
        t_sq2 = p.step("scalar", lambda e, _o=ot: e.activation(
            sq[:, 0:512], _o[:, D:D+512], AF.Square), inc=ch_ns)
        p.wait("vector", ch_ns, t_sq2)
        t_r2 = p.step("vector", lambda e: e.reduce_sum(
            ss.ap().rearrange("p (n c) -> p n c", c=1)[:, 16:20, :],
            sq.ap().rearrange("p (n c) -> p n c", n=16)[:, 0:4, :], axis=AX.X),
            inc=ch_n)
        p.wait("vector", ch_n, t_r2)
        t_m = p.step("vector", lambda e: e.tensor_scalar(
            rr[:], ss[:], 1.0 / 128.0, 1.1920928955078125e-07,
            op0=ALU.mult, op1=ALU.add), inc=ch_n)
        p.wait("scalar", ch_n, t_m)
        t_ln2 = p.step("scalar", lambda e: e.activation(rr[:], rr[:], AF.Ln), inc=ch_ns)
        p.wait("scalar", ch_ns, t_ln2)
        t_ex = p.step("scalar", lambda e: e.activation(rr[:], rr[:], AF.Exp, scale=-0.5),
                      inc=ch_ns)
        p.wait("vector", ch_ns, t_ex)
        t_gn = p.step("vector", lambda e: e.tensor_mul(rr[:, 0:16], rr[:, 0:16],
                                                       gn_sb[:]), inc=ch_n)
        p.wait("vector", ch_n, t_gn)

        def apply_norm(e, _o=ot):
            last = None
            for h in range(16):
                last = e.tensor_scalar(_o[:, h*128:(h+1)*128], _o[:, h*128:(h+1)*128],
                                       rr[:, h:h+1], None, op0=ALU.mult)
            for h in range(4):
                last = e.tensor_scalar(_o[:, D+h*128:D+(h+1)*128],
                                       _o[:, D+h*128:D+(h+1)*128],
                                       rr[:, 16+h:17+h], None, op0=ALU.mult)
            return last
        t_an = p.step("vector", apply_norm, inc=ch_n)
        p.wait("vector", ch_n, t_an)

        def rope(e, _o=ot, _i=i):
            last = None
            for h in range(20):
                base = h * 128 if h < 16 else D + (h - 16) * 128
                x1 = _o[:, base:base+32]
                x2 = _o[:, base+32:base+64]
                e.tensor_mul(t0[:], x1[:, :] if False else x1, cos_sb[_i][:])
                e.tensor_mul(t1[:], x2, sin_sb[_i][:])
                e.tensor_mul(t2[:], x1, sin_sb[_i][:])
                e.tensor_mul(t3[:], x2, cos_sb[_i][:])
                e.tensor_add(x1, t0[:], t1[:])
                last = e.tensor_sub(x2, t3[:], t2[:])
            return last
        t_rope = p.step("vector", rope, inc=ch_n)
        p.wait("gpsimd", ch_n, t_rope)
        p.dma("gpsimd", q_o[i*128:(i+1)*128, :], ot[:, 0:D], ch_st[i % 2])
        p.dma("gpsimd", k_o[i*128:(i+1)*128, :], ot[:, D:D+512], ch_st[i % 2])
        p.dma("gpsimd", v_o[i*128:(i+1)*128, :], ot[:, D+512:OW], ch_st[i % 2])
    p.wait("gpsimd", ch_st[0], ch_st[0].n)
    p.wait("gpsimd", ch_st[1], ch_st[1].n)
    return p.finalize()



@lru_cache(maxsize=None)
def build_gqa_post():
    p = Prog("gqa_post")
    y_in = p.dram_in("y", (TPC, D), BF16)
    v_in = p.dram_in("v", (TPC, 512), BF16)
    yo = p.dram_out("yo", (TPC, D), BF16)
    yt = [p.sbuf(f"yt{j}", (128, D)) for j in range(2)]
    vt = [p.sbuf(f"vt{j}", (128, 512)) for j in range(2)]
    vn = p.sbuf("vn", (128, 512))
    sq = p.sbuf("sqp", (128, 512))
    ssv = p.sbuf("ssv", (128, 4))
    dd = p.sbuf("dd", (128, 16))
    scr = p.sbuf("scr", (128, 128))
    ob = [p.sbuf(f"ob{j}", (128, D), BF16) for j in range(2)]
    ch_ld = [p.chan("ld0"), p.chan("ld1")]
    ch_v = p.chan("v"); ch_s = p.chan("s")
    ch_st = [p.chan("st0"), p.chan("st1")]

    don_toks = []
    for i in range(NT):
        b = i % 2
        if i >= 2:
            p.wait("gpsimd", ch_v, don_toks[i - 2])
        p.dma("gpsimd", yt[b][:], y_in[i*128:(i+1)*128, :], ch_ld[b])
        t_l = p.dma("gpsimd", vt[b][:], v_in[i*128:(i+1)*128, :], ch_ld[b])
        p.wait("scalar", ch_ld[b], t_l)
        t_sq = p.step("scalar", lambda e, _b=b: e.activation(
            sq[:], vt[_b][:], AF.Square), inc=ch_s)
        p.wait("vector", ch_s, t_sq)
        t_r = p.step("vector", lambda e: e.reduce_sum(
            ssv.ap().rearrange("p (n c) -> p n c", c=1),
            sq.ap().rearrange("p (n c) -> p n c", n=4), axis=AX.X), inc=ch_v)
        p.wait("vector", ch_v, t_r)
        t_mx = p.step("vector", lambda e: e.tensor_scalar_max(ssv[:], ssv[:], 1e-24),
                      inc=ch_v)
        p.wait("scalar", ch_v, t_mx)
        t_ln = p.step("scalar", lambda e: e.activation(ssv[:], ssv[:], AF.Ln), inc=ch_s)
        p.wait("scalar", ch_s, t_ln)
        t_ex = p.step("scalar", lambda e: e.activation(ssv[:], ssv[:], AF.Exp, scale=-0.5),
                      inc=ch_s)
        p.wait("vector", ch_s, t_ex)

        def mkvn(e, _b=b):
            last = None
            for h in range(4):
                last = e.tensor_scalar(vn[:, h*128:(h+1)*128], vt[_b][:, h*128:(h+1)*128],
                                       ssv[:, h:h+1], None, op0=ALU.mult)
            return last
        t_vn = p.step("vector", mkvn, inc=ch_v)
        p.wait("vector", ch_v, t_vn)

        def dots(e, _b=b):
            last = None
            for h in range(16):
                kv = h // 4
                e.tensor_mul(scr[:], yt[_b][:, h*128:(h+1)*128],
                             vn[:, kv*128:(kv+1)*128])
                last = e.reduce_sum(dd[:, h:h+1], scr[:], axis=AX.X)
            return last
        t_d = p.step("vector", dots, inc=ch_v)
        p.wait("vector", ch_v, t_d)
        t_nd = p.step("vector", lambda e: e.tensor_scalar_mul(dd[:], dd[:], -1.0),
                      inc=ch_v)
        p.wait("vector", ch_v, t_nd)

        def sub(e, _b=b):
            last = None
            for h in range(16):
                kv = h // 4
                last = e.scalar_tensor_tensor(
                    ob[_b][:, h*128:(h+1)*128], vn[:, kv*128:(kv+1)*128],
                    dd[:, h:h+1], yt[_b][:, h*128:(h+1)*128],
                    op0=ALU.mult, op1=ALU.add)
            return last
        if i >= 2:
            p.wait("vector", ch_st[b], 16 * (i // 2))
        t_sub = p.step("vector", sub, inc=ch_v)
        don_toks.append(t_sub)
        p.wait("sync", ch_v, t_sub)
        p.dma("sync", yo[i*128:(i+1)*128, :], ob[b][:], ch_st[b])
    p.wait("sync", ch_st[0], ch_st[0].n)
    p.wait("sync", ch_st[1], ch_st[1].n)
    return p.finalize()


def gqa_rope_tables():
    inv = 1.0 / 10000.0 ** (np.arange(0, 64, 2, dtype=np.float64) / 64)
    fr = np.arange(S, dtype=np.float64)[:, None] * inv[None, :]
    return (np.ascontiguousarray(np.cos(fr).astype(np.float32)),
            np.ascontiguousarray(np.sin(fr).astype(np.float32)))


def sub_attn_b_L1(x, inp):
    HKV = 4
    nc1 = build_gqa_pre()
    wcat = np.concatenate([np.asarray(inp["L1_attn_b_Wq"]),
                           np.asarray(inp["L1_attn_b_Wk"]),
                           np.asarray(inp["L1_attn_b_Wv"])], axis=1)
    acat = np.concatenate([np.asarray(inp["L1_attn_b_q_A"]),
                           np.asarray(inp["L1_attn_b_k_A"]),
                           np.asarray(inp["L1_attn_b_v_A"])], axis=1)
    bcat = np.zeros((48, 3072), np.float32)
    bcat[0:16, 0:2048] = np.asarray(inp["L1_attn_b_q_B"])
    bcat[16:32, 2048:2560] = np.asarray(inp["L1_attn_b_k_B"])
    bcat[32:48, 2560:3072] = np.asarray(inp["L1_attn_b_v_B"])
    gain = np.tile(np.asarray(inp["L1_attn_b_q_gain"], np.float32)[None, :], (128, 1))
    cos, sin = gqa_rope_tables()
    im = []
    for c in range(NCORES):
        p0 = (c * TPC) % S
        im.append({"x": _f32(x[c*TPC:(c+1)*TPC]), "wcat": _f32(wcat),
                   "acat": _f32(acat), "bcat": bcat, "gain": gain,
                   "cosr": np.ascontiguousarray(cos[p0:p0+TPC]),
                   "sinr": np.ascontiguousarray(sin[p0:p0+TPC]), "ident": IDENT})
    r1 = run(nc1, im).results
    q = np.concatenate([np.asarray(r["q"]) for r in r1])
    k = np.concatenate([np.asarray(r["k"]) for r in r1])
    v = np.concatenate([np.asarray(r["v"]) for r in r1])

    qh = q.reshape(B, S, 16, 128)
    kh = k.reshape(B, S, 4, 128)
    vh = v.reshape(B, S, 4, 128)
    ones_pad = np.zeros((S, 4), dtype=BF)
    ones_pad[:, 0] = 1.0
    nc2 = build_flash(4, 1, 1.0 / 128 ** 0.5)
    im2 = []
    for c in range(NCORES):
        b_, kvh = c // HKV, c % HKV
        qf = np.stack([np.ascontiguousarray(qh[b_, :, kvh*4+j, :].T) for j in range(4)])
        kf = np.ascontiguousarray(kh[b_, :, kvh, :].T)[None]
        vb = np.ascontiguousarray(
            np.concatenate([vh[b_, :, kvh, :], ones_pad], axis=1))[None]
        im2.append({"qf": qf, "kf": kf, "vb": vb, "ident": IDENT, "mask": MASK_TILE})
    r2 = run(nc2, im2).results
    ctx = np.stack([np.asarray(r["ctx"]) for r in r2])
    y = ctx.reshape(B, HKV, 4, S, 128).transpose(0, 3, 1, 2, 4).reshape(T, D)

    nc3 = build_gqa_post()
    im3 = [{"y": np.ascontiguousarray(y[c*TPC:(c+1)*TPC]),
            "v": np.ascontiguousarray(v[c*TPC:(c+1)*TPC])} for c in range(NCORES)]
    r3 = run(nc3, im3).results
    yo = np.concatenate([np.asarray(r["yo"]) for r in r3])

    nc4 = build_projres(D, True)
    im4 = [{"af": np.ascontiguousarray(yo[c*TPC:(c+1)*TPC].T),
            "w": _f32(inp["L1_attn_b_Wproj"]), "x": _f32(x[c*TPC:(c+1)*TPC]),
            "la": _f32(inp["L1_attn_b_p_A"]), "lb": _f32(inp["L1_attn_b_p_B"])}
           for c in range(NCORES)]
    r4 = run(nc4, im4).results
    return np.concatenate([np.asarray(r["xo"]) for r in r4])



def kernel(**inputs):
    inp = {k: np.asarray(v) for k, v in inputs.items()}
    x = np.asarray(inp["L0_attn_a_x"], np.float32).reshape(T, D)

    x = sub_attn_a(x, inp["L0_attn_a_Wq"], inp["L0_attn_a_Wk"],
                   inp["L0_attn_a_Wv"], inp["L0_attn_a_Wo"], True, 1.0 / 128 ** 0.5)
    qw1t, qw2t, qw3t = quantize_host_launch(inp["L0_ffn_w1"], inp["L0_ffn_w2"],
                                            inp["L0_ffn_w3"])
    x = sub_ffn(x, qw1t, qw2t, qw3t)
    x = sub_attn_b_L0(x, inp)
    x = sub_ffn(x, qw1t, qw2t, qw3t)

    x = sub_attn_a(x, inp["L1_attn_a_Wq"], inp["L1_attn_a_Wk"],
                   inp["L1_attn_a_Wv"], inp["L1_attn_a_Wo"], False, 128.0 ** 0.5)
    x = sub_moe(x, inp)
    x = sub_attn_b_L1(x, inp)
    x = sub_moe(x, inp)
    return x.reshape(B, S, D)
